# version 26 (final)
# baseline (speedup 1.0000x reference)
import math
import numpy as np
from contextlib import ExitStack

import concourse.bass as bass
import concourse.mybir as mybir
from concourse.bass_utils import run_bass_kernel_spmd

F32 = mybir.dt.float32
BF16 = mybir.dt.bfloat16
AF = mybir.ActivationFunctionType
ALU = mybir.AluOpType
AX = mybir.AxisListType

MASKV = -30000.0
NCORES = 8


def host_constants(T):
    C = {}
    C["ident"] = np.eye(128, dtype=np.float32)
    r = np.arange(128)
    C["mcaus"] = np.where(r[:, None] >= r[None, :], 0.0, MASKV).astype(np.float32)
    C["m_l1a2"] = np.where(r[:, None] < r[None, :], 0.0, MASKV).astype(np.float32)
    C["m_l1f4"] = np.where(r[:, None] <= r[None, :], 0.0, MASKV).astype(np.float32)
    NCH = T // 512
    E = np.zeros((128, NCH * NCH), dtype=np.float32)
    for c in range(NCH):
        E[:, c * NCH + c] = 1.0
    C["esum"] = E
    selp = np.zeros((NCH, NCH * 128), dtype=np.float32)
    seln = np.zeros((NCH, NCH * 128), dtype=np.float32)
    for c in range(NCH):
        selp[c, c * 128:(c + 1) * 128] = 1.0
        seln[c, c * 128:(c + 1) * 128] = -1.0
    C["selp"], C["seln"] = selp, seln
    import ml_dtypes
    s32 = np.zeros((32, 32 * 128), dtype=np.float32)
    for e in range(32):
        s32[e, e * 128:(e + 1) * 128] = 1.0
    C["sel32"] = s32
    C["sel32b"] = s32.astype(ml_dtypes.bfloat16)
    head = (r // 32) % 2
    C["uvind"] = (head[:, None] == head[None, :]).astype(np.float32)
    DH, WIN = 64, 512
    d = np.arange(0, DH, 2, dtype=np.float64)
    inv = 1.0 / (10000.0 ** (d / DH))
    t = np.arange(T, dtype=np.float64)
    fr = t[:, None] * inv[None, :]
    base = (d + 0.4 * DH) / (1.4 * DH)
    power = (t - T // 2) / float(WIN // 2)
    sc = base[None, :] ** power[:, None]
    cos, sin = np.cos(fr), np.sin(fr)
    def tile4(a):
        a = a.T.astype(np.float32)
        return np.concatenate([a, a, a, a], axis=0)
    sgn = np.where(r[:, None] < 64, -1.0, 1.0).astype(np.float32)
    C["csq"] = tile4(cos * sc)
    C["ssq"] = sgn * tile4(sin * sc)
    C["csk"] = tile4(cos / sc)
    C["ssk"] = sgn * tile4(sin / sc)
    return C


def pack_weights(inp):
    W = {}
    W["l0_wq"] = inp["L0_attn_Wq"]
    W["l0_wk"] = inp["L0_attn_Wk"]
    W["l0_wvT"] = inp["L0_attn_Wv"].T
    W["l0_woT"] = inp["L0_attn_Wo"].T
    W["gwT"] = inp["L0_ffn_gate_w"].T
    W["guwT"] = inp["L0_ffn_gate_up_w"].transpose(0, 2, 1)
    W["dwT"] = inp["L0_ffn_down_w"].transpose(0, 2, 1)
    W["l1a_wqkv"] = inp["L1_attn_W_qkv"]
    W["l1a_bqkv"] = inp["L1_attn_b_qkv"]
    W["l1a_wout"] = inp["L1_attn_W_out"]
    W["l1a_bout"] = inp["L1_attn_b_out"]
    perm = []
    for p in range(4):
        for (h, d0) in ((2 * p, 0), (2 * p + 1, 0), (2 * p, 32), (2 * p + 1, 32)):
            perm += [h * 64 + d0 + j for j in range(32)]
    perm = np.array(perm)
    wq = inp["L1_ffn_wqkv"]
    W["l1f_wqkv"] = np.concatenate(
        [wq[:, :, perm], wq[:, :, 512 + perm], wq[:, :, 1024:]], axis=2)
    qs, ks = inp["L1_ffn_q_scale"], inp["L1_ffn_k_scale"]
    W["l1f_qscol"] = np.concatenate(
        [qs[:, :32], qs[:, :32], qs[:, 32:], qs[:, 32:]], axis=1)
    W["l1f_kscol"] = np.concatenate(
        [ks[:, :32], ks[:, :32], ks[:, 32:], ks[:, 32:]], axis=1)
    W["l1f_wo"] = inp["L1_ffn_wo"]
    W["l1f_lng"] = inp["L1_ffn_ln_g"]
    W["l1f_lnb"] = inp["L1_ffn_ln_b"]
    HID, HP = 2730, 2816
    w1 = inp["L1_ffn_w1"]
    w1p = np.zeros((2, 1024, 2 * HP), dtype=np.float32)
    w1p[:, :, :HID] = w1[:, :, :HID]
    w1p[:, :, HP:HP + HID] = w1[:, :, HID:]
    W["l1f_w1"] = w1p
    b1 = inp["L1_ffn_b1"]
    b1a = np.zeros((2, HP), dtype=np.float32); b1a[:, :HID] = b1[:, :HID]
    b1g = np.zeros((2, HP), dtype=np.float32); b1g[:, :HID] = b1[:, HID:]
    W["l1f_b1a"], W["l1f_b1g"] = b1a, b1g
    w2 = inp["L1_ffn_w2"]
    w2p = np.zeros((2, HP, 1024), dtype=np.float32)
    w2p[:, :HID] = w2
    W["l1f_w2"] = w2p
    W["l1f_b2"] = inp["L1_ffn_b2"]
    return {k: np.ascontiguousarray(np.asarray(v, dtype=np.float32)) for k, v in W.items()}



ENGS = ("sp", "g", "pe", "dve", "act")

class BB:
    def __init__(self, nc, stack):
        self.nc = nc
        self._stack = stack
        self.q = {e: [] for e in ENGS}
        self.sem = {}
        for name in ("S_sp", "S_g", "S_pe", "S_dve", "S_act", "D_sp", "D_g"):
            self.sem[name] = stack.enter_context(nc.semaphore(name))
        self.cnt = {name: 0 for name in self.sem}
        self.esem = {"sp": "S_sp", "g": "S_g", "pe": "S_pe",
                     "dve": "S_dve", "act": "S_act"}
        self.auto = {"dve", "act"}
        self.bar_toks = []
        self.waited = {e: {} for e in ENGS}

    def _filt(self, eng, wl):
        seen = self.waited[eng]
        out = []
        for (n, v) in wl:
            if seen.get(n, 0) < v:
                seen[n] = v
                out.append((n, v))
        return tuple(out)

    def tok(self, eng):
        s = self.esem[eng]
        return (s, self.cnt[s])

    def dtok(self, eng):
        s = "D_" + eng
        return (s, self.cnt[s])

    def _coalesce(self, waits):
        m = {}
        for (n, v) in waits:
            if n.startswith("D_"):
                v = self.cnt[n]
            if v > 0 and v > m.get(n, 0):
                m[n] = v
        return tuple(m.items())

    def op(self, eng, fn, waits=(), inc=True):
        s = self.esem[eng]
        wl = list(waits)
        if eng in self.auto:
            if self.cnt[s] > 0:
                wl.append((s, self.cnt[s]))
            inc = True
        wl = self._filt(eng, self._coalesce(wl))
        if inc:
            self.cnt[s] += 1
        val = self.cnt[s]
        sems = self.sem

        def run(E, fn=fn, wl=wl, inc=inc, s=s):
            for (n, v) in wl:
                E.wait_ge(sems[n], v)
            r = fn(E)
            if inc:
                r.then_inc(sems[s], 1)
        self.q[eng].append(run)
        return (s, val)

    def chan(self, name):
        key = "D_" + name
        if key not in self.sem:
            self.sem[key] = self._stack.enter_context(self.nc.semaphore(key))
            self.cnt[key] = 0
        return key

    def dma(self, eng, out, in_, waits=(), with_bar=True, ch=None):
        ds = "D_" + eng if ch is None else self.chan(ch)
        wl = list(waits)
        if with_bar:
            wl += self.bar_toks
        wl = self._filt(eng, self._coalesce(wl))
        self.cnt[ds] += 16
        val = self.cnt[ds]
        sems = self.sem

        def run(E, out=out, in_=in_, wl=wl, ds=ds):
            for (n, v) in wl:
                E.wait_ge(sems[n], v)
            E.dma_start(out=out, in_=in_).then_inc(sems[ds], 16)
        self.q[eng].append(run)
        return (ds, val)

    def wait(self, eng, waits):
        wl = self._filt(eng, self._coalesce(waits))
        sems = self.sem

        def run(E, wl=wl):
            for (n, v) in wl:
                E.wait_ge(sems[n], v)
        self.q[eng].append(run)

    def barrier(self):
        toks = []
        for e in ENGS:
            if e in ("pe", "dve", "act"):
                toks.append(self.op(e, lambda E: E.drain()))
            else:
                toks.append(self.op(e, lambda E: E.nop()))
        toks += [self.dtok("sp"), self.dtok("g")]
        toks += [(n, self.cnt[n]) for n in self.cnt if n.startswith("D_xb")]
        for e in ENGS:
            self.wait(e, toks)
        self.bar_toks = toks
        return toks

    def flush(self, block):
        decs = {"sp": block.sync, "g": block.gpsimd, "pe": block.tensor,
                "dve": block.vector, "act": block.scalar}
        for e in ENGS:
            ops = self.q[e]
            def body(E, ops=ops):
                for f in ops:
                    f(E)
            decs[e](body)


def mm(b, out, lhsT, rhs, waits=(), start=True, stop=True, inc=True, tp=None):
    def fn(E, out=out, lhsT=lhsT, rhs=rhs, start=start, stop=stop, tp=tp):
        kw = {}
        if tp is not None:
            kw["tile_position"] = tp
        return E.matmul(out, lhsT, rhs, start=start, stop=stop,
                        skip_group_check=True, **kw)
    return b.op("pe", fn, waits=waits, inc=inc)


def transp(b, out, in_, ident, waits=(), inc=True):
    return b.op("pe", lambda E: E.transpose(out, in_, ident), waits=waits, inc=inc)


class S:
    pass


_UC = [0]
DEBUG_D = [False]

def _u(name):
    _UC[0] += 1
    return f"{name}_{_UC[0]}"



def declare_dram(nc, blocks, T, NSEQ):
    D = {}
    def di(name, shape):
        D[name] = nc.dram_tensor(name, list(shape), F32, kind="ExternalInput").ap()
    di("xt", (NSEQ, 1024, T))
    D["yt"] = nc.dram_tensor("yt", [NSEQ, 1024, T], F32, kind="ExternalOutput").ap()
    di("ident", (128, 128))
    di("mcaus", (128, 128))
    NCH = T // 512
    di("esum", (128, NCH * NCH))
    di("selp", (NCH, NCH * 128))
    di("seln", (NCH, NCH * 128))
    if "A" in blocks:
        di("l0_wq", (64, 64)); di("l0_wk", (64, 64))
        di("l0_wvT", (64, 64)); di("l0_woT", (1024, 1024))
    if "B" in blocks:
        di("gwT", (1024, 32))
        D["sel32b"] = nc.dram_tensor("sel32b", [32, 32 * 128], BF16,
                                     kind="ExternalInput").ap()
        di("guwT", (32, 1024, 1024)); di("dwT", (32, 512, 1024))
    if "C" in blocks:
        di("m_l1a2", (128, 128))
        di("l1a_wqkv", (1024, 3072)); di("l1a_bqkv", (3072,))
        di("l1a_wout", (1024, 1024)); di("l1a_bout", (1024,))
    if "D" in blocks:
        di("m_l1f4", (128, 128)); di("uvind", (128, 128))
        di("csq", (128, T)); di("ssq", (128, T))
        di("csk", (128, T)); di("ssk", (128, T))
        di("l1f_wqkv", (2, 1024, 1536))
        di("l1f_qscol", (2, 128)); di("l1f_kscol", (2, 128))
        di("l1f_wo", (2, 512, 1024))
        di("l1f_lng", (2, 1024)); di("l1f_lnb", (2, 1024))
        di("l1f_w1", (2, 1024, 2 * 2816))
        di("l1f_b1a", (2, 2816)); di("l1f_b1g", (2, 2816))
        di("l1f_w2", (2, 2816, 1024)); di("l1f_b2", (2, 1024))
        D["xin_spill"] = nc.dram_tensor(
            "xin_spill", [8, 128, T], F32, kind="Internal").ap()
        if DEBUG_D[0]:
            D["dbg_q"] = nc.dram_tensor("dbg_q", [4, 128, T], F32,
                                        kind="ExternalOutput").ap()
            D["dbg_k"] = nc.dram_tensor("dbg_k", [4, 128, T], F32,
                                        kind="ExternalOutput").ap()
            D["dbg_o"] = nc.dram_tensor("dbg_o", [4, 128, T], F32,
                                        kind="ExternalOutput").ap()
            D["dbg_v"] = nc.dram_tensor("dbg_v", [128, 16 * 512], F32,
                                        kind="ExternalOutput").ap()
    return D


def alloc_persistent(b, stack, D, T, blocks):
    nc = b.nc
    st = S()
    st.T = T
    st.NCH = T // 512
    st.NQT = T // 128
    sb = lambda name, shape, dt: stack.enter_context(
        nc.sbuf_tensor(name, list(shape), dt))
    st.sb = sb
    st.xf = [sb(f"xf{i}", [128, T], F32) for i in range(8)]
    st.xbf = [sb(f"xbf{i}", [128, T], BF16) for i in range(8)]
    st.idf = sb("idf", [128, 128], F32)
    st.idb = sb("idb", [128, 128], BF16)
    st.mcaus = sb("mcausS", [128, 128], F32)
    st.esum = sb("esumS", [128, st.NCH * st.NCH], BF16)
    st.selp = sb("selpS", [st.NCH, st.NCH * 128], F32)
    st.seln = sb("selnS", [st.NCH, st.NCH * 128], F32)
    st.xnb = sb("xnb", [128, 512], BF16)
    st.xnb2 = sb("xnb2", [128, 512], BF16)
    st.sqa = sb("sqa", [128, 512], BF16)
    st.sqa2 = sb("sqa2", [128, 512], BF16)
    st.stat = sb("stat", [st.NCH, 512], F32)
    st.stat2 = sb("stat2", [st.NCH, 512], F32)
    st.stat3 = sb("stat3", [st.NCH, 512], F32)
    st.scr512 = sb("scr512", [128, 512], F32)
    st.eps = sb("eps", [128, 1], F32)
    st.mcb = sb("mcausB", [128, 128], BF16)
    if "C" in blocks:
        st.m_l1a2 = sb("ml1a2S", [128, 128], F32)
        st.ml1a2b = sb("ml1a2B", [128, 128], BF16)
    if "D" in blocks:
        st.m_l1f4 = sb("ml1f4S", [128, 128], F32)
        st.ml1f4b = sb("ml1f4B", [128, 128], BF16)
        st.uvind = sb("uvindS", [128, 128], BF16)
    teps = b.op("dve", lambda E: E.memset(st.eps[:], 1e-5))
    tl = [teps]
    tl.append(b.dma("sp", st.idf[:], D["ident"][:]))
    tl.append(b.dma("g", st.idb[:], D["ident"][:]))
    tl.append(b.dma("sp", st.mcaus[:], D["mcaus"][:]))
    tl.append(b.dma("g", st.esum[:], D["esum"][:]))
    tl.append(b.dma("sp", st.selp[:], D["selp"][:]))
    tl.append(b.dma("sp", st.seln[:], D["seln"][:]))
    tl.append(b.dma("g", st.mcb[:], D["mcaus"][:]))
    if "C" in blocks:
        tl.append(b.dma("sp", st.m_l1a2[:], D["m_l1a2"][:]))
        tl.append(b.dma("g", st.ml1a2b[:], D["m_l1a2"][:]))
    if "D" in blocks:
        tl.append(b.dma("sp", st.m_l1f4[:], D["m_l1f4"][:]))
        tl.append(b.dma("g", st.ml1f4b[:], D["m_l1f4"][:]))
        tl.append(b.dma("g", st.uvind[:], D["uvind"][:]))
    st.const_toks = tl
    return st



def emit_ln(b, st, ps, mode="std", gb=None, ybf=None, src_fn=None, extra=None):
    NCH = st.NCH
    psA, psB = ps[:, 0:512], ps[:, 512:1024]
    psMs = [ps[:, 1024:1536]]
    psRs = [ps[:, 1536:2048]]
    if extra is not None:
        psMs.append(extra[0])
        psRs.append(extra[1])
    else:
        psMs.append(psMs[0])
        psRs.append(psRs[0])
    t2 = None
    prev_mmB = [None, None]
    xnbs = [st.xnb, st.xnb2]
    sqas = [st.sqa, st.sqa2]
    for c in range(NCH):
        for i in range(8):
            par = i % 2
            xnb, sqa = xnbs[par], sqas[par]
            wprev = [prev_mmB[par]] if prev_mmB[par] else []
            if mode == "inner":
                rhs_sum = src_fn(i, c)
                tc = None
            else:
                tc = b.op("act", lambda E, i=i, c=c, xnb=xnb: E.activation(
                    out=xnb[:], in_=st.xf[i][:, c * 512:(c + 1) * 512],
                    func=AF.Copy), waits=wprev)
                rhs_sum = xnb[:]
            tsq = b.op("act", lambda E, r=rhs_sum, sqa=sqa: E.activation(
                out=sqa[:], in_=r, func=AF.Square), waits=wprev)
            w1 = [tc] if tc else []
            mm(b, psA[0:NCH, :], st.esum[:, c * NCH:(c + 1) * NCH], rhs_sum,
               waits=w1, start=(i == 0 and c == 0), stop=(i == 7 and c == NCH - 1),
               inc=False)
            t2 = mm(b, psB[0:NCH, :], st.esum[:, c * NCH:(c + 1) * NCH], sqa[:],
                    waits=[tsq], start=(i == 0 and c == 0),
                    stop=(i == 7 and c == NCH - 1), inc=True)
            prev_mmB[par] = t2
    c1 = 1.0 / 1024.0
    tm = b.op("dve", lambda E: E.tensor_scalar_mul(st.stat[0:NCH, :], psA[0:NCH, :], c1),
              waits=[t2])
    te = b.op("dve", lambda E: E.tensor_scalar_mul(st.stat3[0:NCH, :], psB[0:NCH, :], c1))
    tq = b.op("dve", lambda E: E.tensor_mul(st.stat2[0:NCH, :], st.stat[0:NCH, :],
                                            st.stat[0:NCH, :]))
    tv = b.op("dve", lambda E: E.tensor_sub(st.stat3[0:NCH, :], st.stat3[0:NCH, :],
                                            st.stat2[0:NCH, :]))
    ts = b.op("act", lambda E: E.activation(out=st.stat2[0:NCH, :],
                                            in_=st.stat3[0:NCH, :], func=AF.Sqrt,
                                            bias=st.eps[0:NCH, :]), waits=[tv])
    tr = b.op("dve", lambda E: E.reciprocal(st.stat2[0:NCH, :], st.stat2[0:NCH, :]),
              waits=[ts])
    out_toks = []
    last_apply = [None, None]
    for c in range(NCH):
        par = c % 2
        psM, psR = psMs[par], psRs[par]
        wb = [tm, tr] + ([last_apply[par]] if last_apply[par] else [])
        tbm = mm(b, psM, st.seln[0:NCH, c * 128:(c + 1) * 128], st.stat[0:NCH, :],
                 waits=wb)
        tbr = mm(b, psR, st.selp[0:NCH, c * 128:(c + 1) * 128], st.stat2[0:NCH, :],
                 waits=[tbm])
        for i in range(8):
            if mode == "inner":
                ta = b.op("dve", lambda E, s_=src_fn(i, c), psM=psM: E.tensor_add(
                    st.scr512[:], s_, psM), waits=[tbm])
                tb = b.op("dve", lambda E, psR=psR: E.tensor_mul(
                    st.scr512[:], st.scr512[:], psR), waits=[tbr])
                g, bb_ = gb
                tg = b.op("dve", lambda E, i=i, c=c, g=g, bb_=bb_: E.tensor_scalar(
                    ybf[i][:, c * 512:(c + 1) * 512], st.scr512[:],
                    g[:, i:i + 1], bb_[:, i:i + 1], ALU.mult, ALU.add))
                out_toks.append(tg)
                last_apply[par] = tg
            else:
                ta = b.op("dve", lambda E, i=i, c=c, psM=psM: E.tensor_add(
                    st.xf[i][:, c * 512:(c + 1) * 512],
                    st.xf[i][:, c * 512:(c + 1) * 512], psM), waits=[tbm])
                tb = b.op("dve", lambda E, i=i, c=c, psR=psR: E.tensor_mul(
                    st.xf[i][:, c * 512:(c + 1) * 512],
                    st.xf[i][:, c * 512:(c + 1) * 512], psR), waits=[tbr])
                tcst = b.op("act", lambda E, i=i, c=c: E.activation(
                    out=st.xbf[i][:, c * 512:(c + 1) * 512],
                    in_=st.xf[i][:, c * 512:(c + 1) * 512], func=AF.Copy),
                    waits=[tb])
                out_toks.append(tcst)
                last_apply[par] = tb
    return out_toks



def emit_block_A(b, st, D, stack, s):
    nc = b.nc
    T, NCH, NQT = st.T, st.NCH, st.NQT
    with ExitStack() as sc:
        e = sc.enter_context
        a64 = e(nc.sbuf_tensor(_u("a64"), [128, 64], BF16))
        wv2 = e(nc.sbuf_tensor(_u("wv2"), [128, 64], BF16))
        wqs = e(nc.sbuf_tensor(_u("wqs"), [64, 64], BF16))
        wks = e(nc.sbuf_tensor(_u("wks"), [64, 64], BF16))
        zbf = [e(nc.sbuf_tensor(_u(f"zbf{j}"), [128, T], BF16)) for j in range(2)]
        vtok = [e(nc.sbuf_tensor(_u(f"vtok{j}"), [128, T], BF16)) for j in range(2)]
        pbfs = [e(nc.sbuf_tensor(_u(f"pbf{j}"), [128, T], BF16)) for j in range(2)]
        pts = [e(nc.sbuf_tensor(_u(f"pts{j}"), [128, 1024], BF16)) for j in range(2)]
        avT = [e(nc.sbuf_tensor(_u(f"avT{i}"), [128, T], BF16)) for i in range(8)]
        woT = e(nc.sbuf_tensor(_u("woT"), [128, 8 * 1024], BF16))
        nm = e(nc.sbuf_tensor(_u("nm"), [128, 1], F32))
        rs4 = e(nc.sbuf_tensor(_u("rs4"), [128, 4], F32))
        rs = e(nc.sbuf_tensor(_u("rs"), [128, 1], F32))
        rr = e(nc.sbuf_tensor(_u("rr"), [128, 1], F32))
        ps = e(nc.psum_tensor(_u("psA"), [128, 2048], F32))
        ps_t = [e(nc.psum_tensor(_u(f"psT{j}"), [128, 1024], BF16)) for j in range(2)]
        ps_o = e(nc.psum_tensor(_u("psO"), [128, 512], F32))
        ps_m = e(nc.psum_tensor(_u("psMZ"), [128, 512], F32))

        t1 = b.dma("g", wqs[:], D["l0_wq"][:])
        t2 = b.dma("g", wks[:], D["l0_wk"][:])
        t3 = b.dma("g", wv2[0:64, :], D["l0_wvT"][:])
        t4 = b.dma("g", wv2[64:128, :], D["l0_wvT"][:])
        twoT = b.dma("g", woT[:].rearrange("p (k f) -> p k f", k=8),
                     D["l0_woT"].rearrange("(k p) f -> p k f", p=128))
        if getattr(st, "issue_xbf", None) is not None:
            st.xbf_tok = st.issue_xbf()
            st.xbf_ready = list(st.xbf_tok)
            st.issue_xbf = None
        tA = mm(b, ps_m[0:64, 0:64], wqs[:], wks[:], waits=[t1, t2])
        tA1 = b.op("dve", lambda E: E.tensor_scalar_mul(
            a64[0:64, :], ps_m[0:64, 0:64], 0.125), waits=[tA])
        tA2 = b.op("dve", lambda E: E.tensor_scalar_mul(
            a64[64:128, :], ps_m[0:64, 0:64], 0.125))

        pass
        bank_exp = [None] * 4
        prev_pt_tok = [None, None]
        pendA = [None]

        def flush_tavA():
            state = pendA[0]
            if state is None:
                return
            p_, hb_, qt_, tnrm_, vt_ = state
            pendA[0] = None
            pbf = pbfs[qt_ % 2]
            avmm = None
            nkb = qt_ + 1
            bats = [(s, min(s + 4, nkb)) for s in range(0, nkb, 4)]
            tevs = []
            for bi, (b0, b1) in enumerate(bats):
                tsl = bi % 2
                ttr = None
                for i in range(b0, b1):
                    tw = [tnrm_]
                    if i == b0 and prev_pt_tok[tsl] is not None:
                        tw.append(prev_pt_tok[tsl])
                    ttr = transp(b, ps_t[tsl][:, (i - b0) * 128:(i - b0 + 1) * 128],
                                 pbf[:, i * 128:(i + 1) * 128], st.idb[:],
                                 waits=tw, inc=(i == b1 - 1))
                    pbf_rdA[qt_ % 2] = ttr
                nb = (b1 - b0) * 128
                if (bi % 2) == 0:
                    tev = b.op("act", lambda E, tsl=tsl, nb=nb: E.activation(
                        out=pts[tsl][:, 0:nb], in_=ps_t[tsl][:, 0:nb],
                        func=AF.Copy), waits=[ttr])
                else:
                    tev = b.op("dve", lambda E, tsl=tsl, nb=nb: E.tensor_copy(
                        pts[tsl][:, 0:nb], ps_t[tsl][:, 0:nb]), waits=[ttr])
                prev_pt_tok[tsl] = tev
                tevs.append(tev)
            for bi, (b0, b1) in enumerate(bats):
                tsl = bi % 2
                for i in range(b0, b1):
                    avw = [tevs[bi]]
                    if i == 0 and prev_o_tok_l[0] is not None:
                        avw.append(prev_o_tok_l[0])
                    avmm = mm(b, ps_o[0:64, 0:128],
                              vt_[:, i * 128 + 64 * hb_: i * 128 + 64 * hb_ + 64],
                              pts[tsl][:, (i - b0) * 128:(i - b0 + 1) * 128],
                              waits=avw,
                              start=(i == 0), stop=(i == qt_),
                              inc=(i == qt_))
            tev = b.op("dve", lambda E, hb_=hb_, qt_=qt_, p_=p_: E.tensor_copy(
                avT[p_][64 * hb_:64 * hb_ + 64, qt_ * 128:(qt_ + 1) * 128],
                ps_o[0:64, 0:128]), waits=[avmm])
            prev_o_tok_l[0] = tev

        prev_o_tok_l = [None]
        pbf_rdA = [None, None]
        for p in range(8):
            sl = p % 2
            zb, vt = zbf[sl], vtok[sl]
            zw = [st.xbf_tok[p], tA1, tA2, t3, t4]
            if prev_o_tok_l[0] is not None:
                zw.append(prev_o_tok_l[0])
            last_ev = None
            for c in range(NCH):
                for hb in (0, 1):
                    t = mm(b, ps_m[0:64, :], a64[64 * hb:64 * hb + 64, :],
                           st.xbf[p][64 * hb:64 * hb + 64, c * 512:(c + 1) * 512],
                           waits=zw, tp=(64 * hb, 0))
                    tz = b.op("dve", lambda E, hb=hb, c=c, zb=zb: E.tensor_copy(
                        zb[64 * hb:64 * hb + 64, c * 512:(c + 1) * 512],
                        ps_m[0:64, :]), waits=[t])
                    zw = [tz]
                    last_ev = tz
            for tt in range(NQT):
                for hb in (0, 1):
                    t = mm(b, ps_m[:, 0:64],
                           st.xbf[p][64 * hb:64 * hb + 64, tt * 128:(tt + 1) * 128],
                           wv2[64 * hb:64 * hb + 64, :], waits=zw, tp=(64 * hb, 0))
                    tv = b.op("dve", lambda E, tt=tt, hb=hb, vt=vt: E.tensor_copy(
                        vt[:, tt * 128 + 64 * hb: tt * 128 + 64 * hb + 64],
                        ps_m[:, 0:64]), waits=[t])
                    zw = [tv]
                    last_ev = tv
            zv_done = [last_ev]

            for hb in (0, 1):
                h_rows = slice(64 * hb, 64 * hb + 64)
                for qt in range(NQT):
                    keys = (qt + 1) * 128
                    ncb = (keys + 511) // 512
                    cd = ncb - 1
                    order = [cd] + list(range(cd))
                    s_tok = {}
                    w0 = list(zv_done)
                    for c in order:
                        k0 = c * 512
                        k1 = min(k0 + 512, keys)
                        w = w0 + ([bank_exp[c]] if bank_exp[c] else [])
                        w0 = []
                        s_tok[c] = mm(b, ps[:, k0:k1],
                                      zb[h_rows, qt * 128:(qt + 1) * 128],
                                      st.xbf[p][h_rows, k0:k1], waits=w,
                                      tp=(64 * hb, 0),
                                      start=True, stop=(c != cd), inc=True)
                        if c == cd:
                            s_tok[c] = mm(b, ps[:, qt * 128:(qt + 1) * 128],
                                          st.idb[:], st.mcb[:],
                                          waits=[s_tok[c]],
                                          start=False, stop=True, inc=True)
                    flush_tavA()
                    tmax = b.op("dve", lambda E, qt=qt: E.reduce_max(
                        nm[:], ps[:, qt * 128:(qt + 1) * 128], axis=AX.X,
                        negate=True), waits=[s_tok[cd]])
                    texp = {}
                    pbf = pbfs[qt % 2]
                    for c in order:
                        k0 = c * 512
                        k1 = min(k0 + 512, keys)
                        we = [tmax, s_tok[c]]
                        if pbf_rdA[qt % 2] is not None:
                            we.append(pbf_rdA[qt % 2])
                        texp[c] = b.op("act", lambda E, k0=k0, k1=k1, c=c, pbf=pbf:
                                       E.activation(
                            out=pbf[:, k0:k1], in_=ps[:, k0:k1], func=AF.Exp,
                            bias=nm[:], accum_out=rs4[:, c:c + 1]), waits=we)
                        bank_exp[c] = texp[c]
                    tsum = b.op("dve", lambda E, ncb=ncb: E.reduce_sum(
                        rs[:], rs4[:, 0:ncb], axis=AX.X),
                        waits=[texp[c] for c in order])
                    trr = b.op("dve", lambda E: E.reciprocal(rr[:], rs[:]))
                    tnrm = b.op("dve", lambda E, keys=keys, pbf=pbf: E.tensor_scalar_mul(
                        pbf[:, 0:keys], pbf[:, 0:keys], rr[:]))
                    pendA[0] = (p, hb, qt, tnrm, vt)


        flush_tavA()
        b.barrier()
        bank_rd = [None] * 4
        for c in range(NCH):
            for f in range(8):
                bk = f % 4
                w0 = [twoT] if (f == 0 and c == 0) else []
                if bank_rd[bk] is not None:
                    w0.append(bank_rd[bk])
                dmm = None
                for k in range(8):
                    dmm = mm(b, ps[:, bk * 512:(bk + 1) * 512],
                             woT[:, k * 1024 + f * 128: k * 1024 + f * 128 + 128],
                             avT[k][:, c * 512:(c + 1) * 512],
                             waits=w0, start=(k == 0), stop=(k == 7), inc=(k == 7))
                    w0 = []
                tres = b.op("dve", lambda E, f=f, c=c, bk=bk: E.tensor_add(
                    st.xf[f][:, c * 512:(c + 1) * 512],
                    st.xf[f][:, c * 512:(c + 1) * 512],
                    ps[:, bk * 512:(bk + 1) * 512]), waits=[dmm])
                bank_rd[bk] = tres
        b.barrier()
        emit_ln(b, st, ps, mode="std", extra=(ps_m[:, 0:512], ps_o[:, 0:512]))
        b.barrier()



def build_program(blocks, T=2048, NSEQ=4):
    nc = bass.Bass("TRN2", target_bir_lowering=False, debug=False)
    stack = ExitStack()
    with stack:
        D = declare_dram(nc, blocks, T, NSEQ)
        b = BB(nc, stack)
        blk = stack.enter_context(nc.Block())
        st = alloc_persistent(b, stack, D, T, blocks)
        for s in range(NSEQ):
            lt = []
            for i in range(8):
                lt.append(b.dma("sp", st.xf[i][:],
                                D["xt"][s, i * 128:(i + 1) * 128, :]))

            def issue_xbf(s=s):
                return [b.dma("g", st.xbf[i][:],
                              D["xt"][s, i * 128:(i + 1) * 128, :],
                              ch=f"xb{i}") for i in range(8)]
            if s == 0:
                bt = issue_xbf()
                st.issue_xbf = None
                b.wait("pe", lt + bt + st.const_toks)
                b.barrier()
                st.xbf_tok = [b.bar_toks[0]] * 8
                st.xbf_ready = [b.bar_toks[0]]
            else:
                st.issue_xbf = issue_xbf
            if "A" in blocks:
                emit_block_A(b, st, D, stack, s)
            if "B" in blocks:
                emit_block_B(b, st, D, stack, s)
            if "C" in blocks:
                emit_block_C(b, st, D, stack, s)
            if "D" in blocks:
                emit_block_D(b, st, D, stack, s)
            sto = []
            for i in range(8):
                sto.append(b.dma("sp", D["yt"][s, i * 128:(i + 1) * 128, :],
                                 st.xf[i][:]))
            b.wait("sp", sto)
            b.barrier()
        b.flush(blk)
    return nc, D


def emit_block_B(b, st, D, stack, s):
    nc = b.nc
    T, NCH, NQT = st.T, st.NCH, st.NQT
    CP = min(1024, T)
    NPASS = T // CP
    NSUB = CP // 512
    GRP = 4
    with ExitStack() as sc:
        e = sc.enter_context
        gwTs = e(nc.sbuf_tensor(_u("gwTs"), [128, 8 * 32], BF16))
        lgs = e(nc.sbuf_tensor(_u("lgs"), [32, 512], F32))
        probs = e(nc.sbuf_tensor(_u("probs"), [128, 32], F32))
        max8 = e(nc.sbuf_tensor(_u("max8"), [128, 8], F32))
        resid = e(nc.sbuf_tensor(_u("resid"), [128, 32], F32))
        combf = e(nc.sbuf_tensor(_u("combf"), [128, 32], F32))
        combT = e(nc.sbuf_tensor(_u("combT"), [32, T], BF16))
        selb = [e(nc.sbuf_tensor(_u(f"selb{j}"), [32, 128], BF16)) for j in range(2)]
        nm = e(nc.sbuf_tensor(_u("nmB"), [128, 1], F32))
        rs = e(nc.sbuf_tensor(_u("rsB"), [128, 1], F32))
        rr = e(nc.sbuf_tensor(_u("rrB"), [128, 1], F32))
        guw = [e(nc.sbuf_tensor(_u(f"guw{j}"), [128, 8 * 1024], BF16)) for j in range(2)]
        dwt = [e(nc.sbuf_tensor(_u(f"dwt{j}"), [128, 4 * 512], BF16)) for j in range(4)]
        hst = [[e(nc.sbuf_tensor(_u(f"hst{g}_{jp}"), [128, CP], BF16))
                for jp in range(4)] for g in range(GRP)]
        sgb = st.scr512
        tb = e(nc.sbuf_tensor(_u("tbB"), [128, 512], F32))
        cbsb = e(nc.sbuf_tensor(_u("cbsb"), [128, 1024], BF16))
        ps = e(nc.psum_tensor(_u("psB"), [128, 4096], F32))

        def bank(i):
            return ps[:, i * 512:(i + 1) * 512]

        tgw = b.dma("g", gwTs[:].rearrange("p (k e) -> p k e", k=8),
                    D["gwT"].rearrange("(k p) e -> p k e", p=128))
        selcons = [None, None]

        lgev = None
        lgrd = None
        t_b1rd = None
        t_b2rd = None
        for c in range(NCH):
            lmm = None
            for k in range(8):
                lmm = mm(b, ps[0:32, 0:512], gwTs[:, k * 32:(k + 1) * 32],
                         st.xbf[k][:, c * 512:(c + 1) * 512],
                         waits=[tgw] + st.xbf_ready + ([lgev] if lgev else []),
                         start=(k == 0), stop=(k == 7), inc=(k == 7))
            lgev = b.op("dve", lambda E, c=c: E.tensor_copy(
                lgs[0:32, 0:512], ps[0:32, 0:512]),
                waits=[lmm] + ([lgrd] if lgrd else []))
            for tj in range(4):
                tt = c * 4 + tj
                ttr = transp(b, ps[:, 512:544], lgs[0:32, tj * 128:(tj + 1) * 128],
                             st.idf[0:32, 0:32],
                             waits=[lgev] + ([t_b1rd] if t_b1rd else []))
                lgrd = ttr
                tmax = b.op("dve", lambda E: E.reduce_max(nm[:], ps[:, 512:544],
                                                          axis=AX.X, negate=True),
                            waits=[ttr])
                texp = b.op("act", lambda E: E.activation(
                    out=probs[:], in_=ps[:, 512:544], func=AF.Exp, bias=nm[:]),
                    waits=[tmax])
                t_b1rd = texp
                b.op("dve", lambda E: E.max(out=max8[:], in_=probs[:]), waits=[texp])
                b.op("dve", lambda E: E.match_replace(
                    out=resid[:], in_to_replace=max8[:], in_values=probs[:],
                    imm_value=0.0))
                b.op("dve", lambda E: E.tensor_sub(resid[:], probs[:], resid[:]))
                b.op("dve", lambda E: E.reduce_sum(rs[:], resid[:], axis=AX.X))
                b.op("dve", lambda E: E.reciprocal(rr[:], rs[:]))
                tcm = b.op("dve", lambda E: E.tensor_scalar_mul(
                    combf[:], resid[:], rr[:]))
                tct = transp(b, ps[0:32, 1024:1152], combf[:], st.idf[:],
                             waits=[tcm] + ([t_b2rd] if t_b2rd else []))
                tev = b.op("act", lambda E, tt=tt: E.activation(
                    out=combT[0:32, tt * 128:(tt + 1) * 128],
                    in_=ps[0:32, 1024:1152],
                    func=AF.Copy), waits=[tct])
                t_b2rd = tev
        b.barrier()

        cons_g = [None, None]
        load_g = [None, None]
        cons_dw = [None] * 4
        load_dw_tok = [None] * 4
        bank_rd = [None] * 8
        last_down = [None]

        def load_guw(e_, slot):
            w = [cons_g[slot]] if cons_g[slot] else []
            load_g[slot] = b.dma(
                "g", guw[slot][:].rearrange("p (k j) -> p k j", k=8),
                D["guwT"][e_].rearrange("(k p) j -> p k j", p=128),
                waits=w, ch=f"guw{slot}")

        def load_dw(e_, half):
            sl = e_ % 4
            w = [cons_dw[sl]] if cons_dw[sl] else []
            load_dw_tok[sl] = b.dma(
                "g", dwt[sl][:].rearrange("p (jp f) -> p jp f", jp=4),
                D["dwT"][e_].rearrange("(jp p) f -> p jp f", p=128)
                [:, :, half * 512:(half + 1) * 512],
                waits=w, ch=f"dwt{sl}")

        first_loads = [False]
        for cp in range(NPASS):
            t0 = cp * CP
            for g0 in range(0, 32, GRP):
                if not first_loads[0]:
                    load_guw(g0 + 0, 0)
                    load_guw(g0 + 1, 1)
                    first_loads[0] = True
                gu_done = []
                for ei in range(GRP):
                    e_ = g0 + ei
                    slot = ei % 2
                    ssl = e_ % 2
                    tsel = b.dma("sp", selb[ssl][:],
                                 D["sel32b"][:, e_ * 128:(e_ + 1) * 128],
                                 waits=[selcons[ssl]] if selcons[ssl] else [],
                                 ch=f"selb{ssl}")
                    cbt = [None] * NSUB
                    for sub in range(NSUB):
                        cbk = sub
                        w = [tsel, t_b2rd]
                        if bank_rd[cbk] is not None:
                            w.append(bank_rd[cbk])
                        tcbm = mm(b, bank(cbk),
                                  selb[ssl][:],
                                  combT[0:32, t0 + sub * 512: t0 + (sub + 1) * 512],
                                  waits=w)
                        selcons[ssl] = tcbm
                        cbt[sub] = b.op("dve", lambda E, sub=sub, cbk=cbk: E.tensor_copy(
                            cbsb[:, sub * 512:(sub + 1) * 512], bank(cbk)),
                            waits=[tcbm])
                        bank_rd[cbk] = cbt[sub]
                    lastpe = None
                    prev_tt = getattr(emit_block_B, "_ptt", None)
                    for jp in range(4):
                        for sub in range(NSUB):
                            rot = (jp * NSUB + sub) % 2
                            bA, bB = rot, 2 + rot
                            tokc = slice(t0 + sub * 512, t0 + (sub + 1) * 512)
                            wA = [load_g[slot]]
                            if bank_rd[bA] is not None:
                                wA.append(bank_rd[bA])
                            gmm = None
                            for k in range(8):
                                gmm = mm(b, bank(bA),
                                         guw[slot][:, k * 1024 + jp * 128:
                                                   k * 1024 + jp * 128 + 128],
                                         st.xbf[k][:, tokc], waits=wA,
                                         start=(k == 0), stop=(k == 7),
                                         inc=(k == 7))
                                wA = []
                            wB = []
                            if bank_rd[bB] is not None:
                                wB.append(bank_rd[bB])
                            umm = None
                            for k in range(8):
                                umm = mm(b, bank(bB),
                                         guw[slot][:, k * 1024 + 512 + jp * 128:
                                                   k * 1024 + 512 + jp * 128 + 128],
                                         st.xbf[k][:, tokc], waits=wB,
                                         start=(k == 0), stop=(k == 7),
                                         inc=(k == 7))
                                wB = []
                            lastpe = umm
                            tsg = b.op("act", lambda E, bA=bA: E.activation(
                                out=sgb[:], in_=bank(bA), func=AF.Silu),
                                waits=[gmm] + ([prev_tt] if prev_tt else []))
                            bank_rd[bA] = tsg
                            tt_ = b.op("dve", lambda E, bB=bB: E.tensor_mul(
                                tb[:], bank(bB), sgb[:]), waits=[umm, tsg])
                            bank_rd[bB] = tt_
                            prev_tt = tt_
                            wth = [cbt[sub]]
                            if last_down[0] is not None:
                                wth.append(last_down[0])
                            th = b.op("dve", lambda E, ei=ei, jp=jp, sub=sub:
                                      E.tensor_mul(
                                hst[ei][jp][:, sub * 512:(sub + 1) * 512],
                                tb[:], cbsb[:, sub * 512:(sub + 1) * 512]),
                                waits=wth)
                            gu_done.append(th)
                    emit_block_B._ptt = prev_tt
                    cons_g[slot] = lastpe
                    load_dw(e_, 0)
                    if ei == 1:
                        load_guw(g0 + 2, 0)
                        load_guw(g0 + 3, 1)
                ng = (g0 + GRP) % 32
                load_guw(ng + 0, 0)
                load_guw(ng + 1, 1)
                sidx = 0
                nsweep = 2 * NSUB
                for fh in range(2):
                    for q in range(2):
                        for sub in range(NSUB):
                            swp_i = q * NSUB + sub
                            par = sidx % 2
                            sidx += 1
                            b4 = 4 + 2 * par
                            sweep_end = None
                            for ei in range(GRP):
                                sl = (g0 + ei) % 4
                                ldw = load_dw_tok[sl]
                                if fh == 0 and swp_i == nsweep - 1 and q + sub > 0:
                                    pass
                                for jp in range(4):
                                    for ft in range(2):
                                        bk = b4 + ft
                                        w = [ldw] + gu_done
                                        if (ei == 0 and jp == 0 and
                                                bank_rd[bk] is not None):
                                            w.append(bank_rd[bk])
                                        last = (ei == GRP - 1 and jp == 3)
                                        sweep_end = mm(
                                            b, bank(bk),
                                            dwt[sl][:, jp * 512 + (2 * q + ft) * 128:
                                                    jp * 512 + (2 * q + ft + 1) * 128],
                                            hst[ei][jp][:, sub * 512:(sub + 1) * 512],
                                            waits=w,
                                            start=(ei == 0 and jp == 0),
                                            stop=last,
                                            inc=(ft == 1 and jp == 3))
                                cons_dw[sl] = sweep_end
                            last_down[0] = sweep_end
                            if fh == 0 and swp_i == nsweep - 1:
                                for ei2 in range(GRP):
                                    load_dw(g0 + ei2, 1)
                            for ft in range(2):
                                fi = 4 * fh + 2 * q + ft
                                bk = b4 + ft
                                tcol = slice(t0 + sub * 512, t0 + (sub + 1) * 512)
                                tres = b.op("dve", lambda E, fi=fi, bk=bk, tcol=tcol:
                                            E.tensor_add(
                                    st.xf[fi][:, tcol], st.xf[fi][:, tcol],
                                    bank(bk)), waits=[sweep_end])
                                bank_rd[bk] = tres
        b.barrier()
        emit_ln(b, st, ps[:, 0:2048], mode="std",
                extra=(ps[:, 2048:2560], ps[:, 2560:3072]))
        b.barrier()


def emit_block_C(b, st, D, stack, s):
    nc = b.nc
    T, NCH, NQT = st.T, st.NCH, st.NQT
    with ExitStack() as sc:
        e = sc.enter_context
        wvr = [e(nc.sbuf_tensor(_u(f"wvr{j}"), [128, 512], BF16)) for j in range(2)]
        wor = [e(nc.sbuf_tensor(_u(f"wor{j}"), [128, 1024], BF16)) for j in range(2)]
        wqp = [e(nc.sbuf_tensor(_u(f"wqp{j}"), [128, 8 * 128], BF16)) for j in range(2)]
        wkp = [e(nc.sbuf_tensor(_u(f"wkp{j}"), [128, 8 * 128], BF16)) for j in range(2)]
        bq = e(nc.sbuf_tensor(_u("bq"), [128, 8], F32))
        bk = e(nc.sbuf_tensor(_u("bk"), [128, 8], F32))
        bv = e(nc.sbuf_tensor(_u("bv"), [128, 8], F32))
        bo = e(nc.sbuf_tensor(_u("bo"), [128, 8], F32))
        vt = e(nc.sbuf_tensor(_u("vt"), [128, NQT * 1024], BF16))
        qbf = e(nc.sbuf_tensor(_u("qbf"), [128, T], BF16))
        kbf = e(nc.sbuf_tensor(_u("kbf"), [128, T], BF16))
        avT = [e(nc.sbuf_tensor(_u(f"avTC{i}"), [128, T], BF16)) for i in range(8)]
        pbf2 = [e(nc.sbuf_tensor(_u(f"pbfC{j}"), [128, 384], BF16)) for j in range(4)]
        pts = [e(nc.sbuf_tensor(_u(f"ptsC{j}"), [128, 384], BF16)) for j in range(2)]
        nm = e(nc.sbuf_tensor(_u("nmC"), [128, 2], F32))
        rs = e(nc.sbuf_tensor(_u("rsC"), [128, 2], F32))
        rr = e(nc.sbuf_tensor(_u("rrC"), [128, 2], F32))

        tb_ = []
        for p in range(8):
            tb_.append(b.dma("sp", bq[:, p:p + 1], D["l1a_bqkv"][p * 128:(p + 1) * 128]))
            tb_.append(b.dma("sp", bk[:, p:p + 1],
                             D["l1a_bqkv"][1024 + p * 128:1024 + (p + 1) * 128]))
            tb_.append(b.dma("sp", bv[:, p:p + 1],
                             D["l1a_bqkv"][2048 + p * 128:2048 + (p + 1) * 128]))
            tb_.append(b.dma("sp", bo[:, p:p + 1], D["l1a_bout"][p * 128:(p + 1) * 128]))

        with (nc.psum_tensor(_u("psC"), [128, 3072], F32) as ps,
              nc.psum_tensor(_u("psbC0"), [128, 1024], BF16) as psb0,
              nc.psum_tensor(_u("psbC1"), [128, 1024], BF16) as psb1):
            psb = [psb0, psb1]

            def bank(i):
                return ps[:, i * 512:(i + 1) * 512]

            vev = [None] * 2
            vcons = [None] * 2
            for tg in range(NQT // 2):
                vmm = [None] * 2
                for k in range(8):
                    tk = None
                    for hf in range(2):
                        sl4 = (2 * k + hf) % 2
                        tw = b.dma("g", wvr[sl4][:],
                                   D["l1a_wqkv"][k * 128:(k + 1) * 128,
                                                 2048 + hf * 512:2048 + (hf + 1) * 512],
                                   ch=f"wvr{sl4}",
                                   waits=[vcons[sl4]] if vcons[sl4] else [])
                        for j in range(2):
                            tt = tg * 2 + j
                            w = [tw] + st.xbf_ready
                            if k == 0 and hf == 0 and vev[j] is not None:
                                w.append(vev[j])
                            tk = mm(b, bank(2 * j + hf),
                                    st.xbf[k][:, tt * 128:(tt + 1) * 128],
                                    wvr[sl4][:],
                                    waits=w, start=(k == 0), stop=(k == 7),
                                    inc=(j == 1 or k == 7))
                            w = []
                            if j == 1 or k == 7:
                                vmm[j] = tk
                        vcons[sl4] = tk
                for j in range(2):
                    tt = tg * 2 + j
                    vev[j] = b.op("dve", lambda E, tt=tt, j=j: E.tensor_copy(
                        vt[:, tt * 1024:(tt + 1) * 1024],
                        ps[:, 2 * j * 512: 2 * j * 512 + 1024]), waits=[vmm[j]])
            b.barrier()

            prev_q = [None, None]
            prev_sc = [None, None]
            prev_pt = [None, None]
            prev_o = [None, None]
            prev_pair_o = None
            pair_cons = [None, None]
            pbf_rd = [None] * 4
            sc_rd = [None] * 4
            pend = [None]

            def flush_tav():
                if pend[0] is None:
                    return
                p, qt, kb0, nkb, tnrm = pend[0]
                pend[0] = None
                par = qt % 2
                tevh = [None, None]
                for hb in (0, 1):
                    tsl = hb
                    ttr = None
                    for i in range(nkb):
                        tw = [tnrm[hb]]
                        if i == 0 and prev_pt[tsl] is not None:
                            tw.append(prev_pt[tsl])
                        ttr = transp(b, psb[tsl][:, i * 128:(i + 1) * 128],
                                     pbf2[2 * hb + par][:, i * 128:(i + 1) * 128],
                                     st.idb[:], waits=tw, inc=(i == nkb - 1))
                    pbf_rd[2 * hb + par] = ttr
                    if hb == 0:
                        tev = b.op("act", lambda E, tsl=tsl, nkb=nkb: E.activation(
                            out=pts[tsl][:, 0:nkb * 128],
                            in_=psb[tsl][:, 0:nkb * 128], func=AF.Copy),
                            waits=[ttr])
                    else:
                        tev = b.op("dve", lambda E, tsl=tsl, nkb=nkb: E.tensor_copy(
                            pts[tsl][:, 0:nkb * 128],
                            psb[tsl][:, 0:nkb * 128]), waits=[ttr])
                    prev_pt[tsl] = tev
                    tevh[hb] = tev
                for hb in (0, 1):
                    h = 2 * p + hb
                    tsl = hb
                    avmm = None
                    for i in range(nkb):
                        kb = kb0 + i
                        avw = [tevh[hb]]
                        if i == 0 and prev_o[hb] is not None:
                            avw.append(prev_o[hb])
                        avmm = mm(b, bank(4 + hb)[0:64, 0:128],
                                  vt[:, kb * 1024 + h * 64:kb * 1024 + h * 64 + 64],
                                  pts[tsl][:, i * 128:(i + 1) * 128],
                                  waits=avw,
                                  start=(i == 0), stop=(i == nkb - 1),
                                  inc=(i == nkb - 1), tp=(0, 0))
                    tevo = b.op("dve", lambda E, hb=hb, qt=qt, p=p: E.tensor_scalar_add(
                        avT[p][64 * hb:64 * hb + 64, qt * 128:(qt + 1) * 128],
                        bank(4 + hb)[0:64, 0:128],
                        bv[64 * hb:64 * hb + 64, p:p + 1]), waits=[avmm])
                    prev_o[hb] = tevo
            for p in range(8):
                sl = p % 2
                wc = [pair_cons[sl]] if pair_cons[sl] else []
                twq = b.dma("g", wqp[sl][:].rearrange("p (k c) -> p k c", k=8),
                            D["l1a_wqkv"].rearrange("(k p) c -> p k c", p=128)
                            [:, :, p * 128:(p + 1) * 128],
                            ch=f"wqp{sl}", waits=wc)
                twk = b.dma("g", wkp[sl][:].rearrange("p (k c) -> p k c", k=8),
                            D["l1a_wqkv"].rearrange("(k p) c -> p k c", p=128)
                            [:, :, 1024 + p * 128:1024 + (p + 1) * 128],
                            ch=f"wkp{sl}", waits=wc)
                qev = None
                kev = None
                for c in range(NCH):
                    w = [twq] + ([prev_q[0]] if prev_q[0] else [])
                    if sc_rd[1] is not None:
                        w.append(sc_rd[1])
                    qmm = None
                    for k in range(8):
                        qmm = mm(b, bank(0), wqp[sl][:, k * 128:(k + 1) * 128],
                                 st.xbf[k][:, c * 512:(c + 1) * 512], waits=w,
                                 start=(k == 0), stop=(k == 7), inc=(k == 7))
                        w = []
                    wq2 = [qmm] + ([prev_pair_o] if prev_pair_o else [])
                    qev = b.op("dve", lambda E, c=c, p=p: E.tensor_scalar(
                        qbf[:, c * 512:(c + 1) * 512], bank(0), bq[:, p:p + 1],
                        0.125, ALU.add, ALU.mult), waits=wq2)
                    prev_q[0] = qev
                    w = [twk] + ([prev_q[1]] if prev_q[1] else [])
                    if sc_rd[3] is not None:
                        w.append(sc_rd[3])
                    kmm = None
                    for k in range(8):
                        kmm = mm(b, bank(1), wkp[sl][:, k * 128:(k + 1) * 128],
                                 st.xbf[k][:, c * 512:(c + 1) * 512], waits=w,
                                 start=(k == 0), stop=(k == 7), inc=(k == 7))
                        w = []
                    wk2 = [kmm] + ([prev_pair_o] if prev_pair_o else [])
                    kev = b.op("dve", lambda E, c=c, p=p: E.tensor_scalar_add(
                        kbf[:, c * 512:(c + 1) * 512], bank(1), bk[:, p:p + 1]),
                        waits=wk2)
                    prev_q[1] = kev
                qk_done = [qev, kev]

                for qt in range(NQT):
                    kb0 = max(0, qt - 2)
                    nkb = qt - kb0 + 1
                    keys = nkb * 128
                    parq = qt % 2
                    smm = [None, None]
                    for hb in (0, 1):
                        hr = slice(64 * hb, 64 * hb + 64)
                        sbk = (2 + hb) if parq == 0 else hb
                        has_band = (qt >= 2)
                        w = qk_done + ([sc_rd[2 * hb + parq]] if sc_rd[2 * hb + parq] else [])
                        tS = mm(b, bank(sbk)[:, 0:keys],
                                qbf[hr, qt * 128:(qt + 1) * 128],
                                kbf[hr, kb0 * 128:(qt + 1) * 128],
                                waits=w, tp=(64 * hb, 0),
                                start=True, stop=False, inc=True)
                        if has_band:
                            tS = mm(b, bank(sbk)[:, 0:128], st.idb[:],
                                    st.ml1a2b[:], waits=[tS],
                                    start=False, stop=False, inc=True)
                        smm[hb] = mm(b, bank(sbk)[:, keys - 128:keys], st.idb[:],
                                     st.mcb[:], waits=[tS],
                                     start=False, stop=True, inc=True)
                    flush_tav()
                    texp = [None, None]
                    tnrm = [None, None]
                    for hb in (0, 1):
                        sbk = (2 + hb) if parq == 0 else hb
                        tmax = b.op("dve", lambda E, sbk=sbk, keys=keys, hb=hb:
                                    E.reduce_max(nm[:, hb:hb + 1],
                                                 bank(sbk)[:, 0:keys],
                                                 axis=AX.X, negate=True),
                                    waits=[smm[hb]])
                        par = qt % 2
                        wexp = [tmax]
                        if pbf_rd[2 * hb + par] is not None:
                            wexp.append(pbf_rd[2 * hb + par])
                        texp[hb] = b.op("act", lambda E, sbk=sbk, keys=keys, hb=hb,
                                        par=par: E.activation(
                            out=pbf2[2 * hb + par][:, 0:keys],
                            in_=bank(sbk)[:, 0:keys],
                            func=AF.Exp, bias=nm[:, hb:hb + 1],
                            accum_out=rs[:, hb:hb + 1]), waits=wexp)
                        sc_rd[2 * hb + par] = texp[hb]
                        b.op("dve", lambda E, hb=hb: E.reciprocal(
                            rr[:, hb:hb + 1], rs[:, hb:hb + 1]), waits=[texp[hb]])
                        tnrm[hb] = b.op("dve", lambda E, keys=keys, hb=hb,
                                        par=par: E.tensor_scalar_mul(
                            pbf2[2 * hb + par][:, 0:keys],
                            pbf2[2 * hb + par][:, 0:keys],
                            rr[:, hb:hb + 1]))
                    pend[0] = (p, qt, kb0, nkb, tnrm)
                flush_tav()
                pair_cons[sl] = prev_o[1]
                prev_pair_o = prev_o[1]
            b.barrier()

        with nc.psum_tensor(_u("psC2"), [128, 4096], F32) as ps2:
            def bank2(i):
                return ps2[:, i * 512:(i + 1) * 512]
            bank_rd = [None] * 8
            kcons = [None, None]
            for c in range(NCH):
                dmm = [None] * 8
                for k in range(8):
                    wk_ = [kcons[k % 2]] if kcons[k % 2] else []
                    two = b.dma("g", wor[k % 2][:],
                                D["l1a_wout"][k * 128:(k + 1) * 128, :],
                                ch=f"wor{k % 2}", waits=wk_)
                    for f in range(8):
                        w = [two]
                        if k == 0 and bank_rd[f] is not None:
                            w.append(bank_rd[f])
                        dmm[f] = mm(b, bank2(f), wor[k % 2][:, f * 128:(f + 1) * 128],
                                    avT[k][:, c * 512:(c + 1) * 512], waits=w,
                                    start=(k == 0), stop=(k == 7),
                                    inc=(k == 7 or f == 7))
                    kcons[k % 2] = dmm[7]
                for f in range(8):
                    tres = b.op("dve", lambda E, f=f, c=c: E.scalar_tensor_tensor(
                        st.xf[f][:, c * 512:(c + 1) * 512], bank2(f), bo[:, f:f + 1],
                        st.xf[f][:, c * 512:(c + 1) * 512], ALU.add, ALU.add),
                        waits=[dmm[f]])
                    bank_rd[f] = tres
            b.barrier()
            emit_ln(b, st, ps2[:, 0:2048], mode="std",
                    extra=(ps2[:, 2048:2560], ps2[:, 2560:3072]))
            b.barrier()




def emit_block_D(b, st, D, stack, s):
    nc = b.nc
    T, NCH, NQT = st.T, st.NCH, st.NQT
    HP = 2816
    CCW = [512] * 5 + [256]
    with ExitStack() as sc:
        e = sc.enter_context
        hsh0 = [e(nc.sbuf_tensor(_u(f"hsh0_{i}"), [128, 512], BF16)) for i in range(4)]
        qsc = e(nc.sbuf_tensor(_u("qsc"), [128, 1], F32))
        qs8 = e(nc.sbuf_tensor(_u("qs8"), [128, 1], F32))
        ksc = e(nc.sbuf_tensor(_u("ksc"), [128, 1], F32))
        lngc = e(nc.sbuf_tensor(_u("lngc"), [128, 8], F32))
        lnbc = e(nc.sbuf_tensor(_u("lnbc"), [128, 8], F32))
        b1ac = e(nc.sbuf_tensor(_u("b1ac"), [128, 22], F32))
        b1gc = e(nc.sbuf_tensor(_u("b1gc"), [128, 22], F32))
        b2c = e(nc.sbuf_tensor(_u("b2c"), [128, 8], F32))

        spill = []
        for i in range(8):
            spill.append(b.dma("sp", D["xin_spill"][i], st.xf[i][:], ch="spill"))

        def shifted_bf(i, c0, w):
            if i < 4:
                return st.xbf[i][:, c0:c0 + w]
            if c0 == 0:
                return hsh0[i - 4][:, 0:w]
            return st.xbf[i][:, c0 - 1:c0 + w - 1]

        def _run_layer(l):
            hsh_t = []
            for i in range(4):
                b.op("dve", lambda E, i=i: E.memset(hsh0[i][:, 0:1], 0.0))
                t2 = b.op("dve", lambda E, i=i: E.tensor_copy(
                    hsh0[i][:, 1:512], st.xbf[4 + i][:, 0:511]))
                hsh_t.append(t2)
            tqs = b.dma("sp", qsc[:], D["l1f_qscol"][l])
            tks = b.dma("sp", ksc[:], D["l1f_kscol"][l])
            tqs8 = b.op("dve", lambda E: E.tensor_scalar_mul(qs8[:], qsc[:], 8.0),
                        waits=[tqs])

            with ExitStack() as sa:
                ea = sa.enter_context
                wqp = [ea(nc.sbuf_tensor(_u(f"wqpD{j}"), [128, 8 * 128], BF16)) for j in range(2)]
                wkp = [ea(nc.sbuf_tensor(_u(f"wkpD{j}"), [128, 8 * 128], BF16)) for j in range(2)]
                wvv = [ea(nc.sbuf_tensor(_u(f"wvvD{j}"), [128, 512], BF16)) for j in range(2)]
                qba = [ea(nc.sbuf_tensor(_u(f"qbaD{p}"), [128, T], BF16)) for p in range(4)]
                kba = [ea(nc.sbuf_tensor(_u(f"kbaD{p}"), [128, T], BF16)) for p in range(4)]
                vt = ea(nc.sbuf_tensor(_u("vtD"), [128, NQT * 512], BF16))
                obf = [ea(nc.sbuf_tensor(_u(f"obfD{i}"), [128, T], BF16)) for i in range(4)]

                nm = ea(nc.sbuf_tensor(_u("nmD"), [128, 2], F32))
                rs = ea(nc.sbuf_tensor(_u("rsD"), [128, 2], F32))
                rr = ea(nc.sbuf_tensor(_u("rrD"), [128, 2], F32))
                rot_scope = ExitStack()
                t3s2 = [rot_scope.enter_context(nc.sbuf_tensor(_u(f"t3sD{j}"), [128, 512], F32))
                        for j in range(2)]
                swp2 = [rot_scope.enter_context(nc.sbuf_tensor(_u(f"swpD{j}"), [128, 512], F32))
                        for j in range(2)]
                tabc = [rot_scope.enter_context(nc.sbuf_tensor(_u("tabcD0"), [128, 512], F32))]
                tabs = [rot_scope.enter_context(nc.sbuf_tensor(_u("tabsD0"), [128, 512], F32))]
                sqas2 = [st.sqa, st.sqa2]
                sp_ = ExitStack()
                ps = sp_.enter_context(nc.psum_tensor(_u("psD"), [128, 3072], F32))
                psb = [sp_.enter_context(nc.psum_tensor(_u(f"psbD{j}"), [128, 1024], BF16))
                       for j in range(2)]

                def bank(i):
                    return ps[:, i * 512:(i + 1) * 512]

                qkcons = [None, None]
                qev = kev = None
                qmm = kmm = None
                for p in range(4):
                    sl = p % 2
                    wc = [qkcons[sl]] if qkcons[sl] else []
                    twq = b.dma("g", wqp[sl][:].rearrange("p (k c) -> p k c", k=8),
                                D["l1f_wqkv"].rearrange("l (k p) c -> l p k c", p=128)
                                [l][:, :, p * 128:(p + 1) * 128], ch=f"wqpD{sl}",
                                waits=wc)
                    twk = b.dma("g", wkp[sl][:].rearrange("p (k c) -> p k c", k=8),
                                D["l1f_wqkv"].rearrange("l (k p) c -> l p k c", p=128)
                                [l][:, :, 512 + p * 128:512 + (p + 1) * 128],
                                ch=f"wkpD{sl}", waits=wc)
                    for c in range(NCH):
                        w = [twq] + st.xbf_ready + hsh_t + ([qev] if qev else [])
                        qmm = None
                        for k in range(8):
                            qmm = mm(b, bank(0), wqp[sl][:, k * 128:(k + 1) * 128],
                                     shifted_bf(k, c * 512, 512), waits=w,
                                     start=(k == 0), stop=(k == 7), inc=(k == 7))
                            w = []
                        qev = b.op("dve", lambda E, p=p, c=c: E.tensor_copy(
                            qba[p][:, c * 512:(c + 1) * 512], bank(0)), waits=[qmm])
                        w = [twk] + ([kev] if kev else [])
                        kmm = None
                        for k in range(8):
                            kmm = mm(b, bank(1), wkp[sl][:, k * 128:(k + 1) * 128],
                                     shifted_bf(k, c * 512, 512), waits=w,
                                     start=(k == 0), stop=(k == 7), inc=(k == 7))
                            w = []
                        kev = b.op("dve", lambda E, p=p, c=c: E.tensor_copy(
                            kba[p][:, c * 512:(c + 1) * 512], bank(1)), waits=[kmm])
                    qkcons[sl] = kmm
                vev = [None] * 4
                vcons = [None, None]
                for tg in range(NQT // 4):
                    tk = None
                    for k in range(8):
                        tw = b.dma("g", wvv[k % 2][:],
                                   D["l1f_wqkv"][l][k * 128:(k + 1) * 128, 1024:1536],
                                   ch=f"wvvD{k % 2}",
                                   waits=[vcons[k % 2]] if vcons[k % 2] else [])
                        for j in range(4):
                            tt = tg * 4 + j
                            w = [tw] + hsh_t
                            if k == 0 and vev[j] is not None:
                                w.append(vev[j])
                            if k < 4:
                                lhs = st.xbf[k][:, tt * 128:(tt + 1) * 128]
                            elif tt == 0:
                                lhs = hsh0[k - 4][:, 0:128]
                            else:
                                lhs = st.xbf[k][:, tt * 128 - 1:(tt + 1) * 128 - 1]
                            tk = mm(b, bank(2 + j), lhs, wvv[k % 2][:],
                                    waits=w, start=(k == 0), stop=(k == 7),
                                    inc=(j == 3 or k == 7))
                            if k == 7:
                                vev[j] = b.op("dve", lambda E, tt=tt, j=j: E.tensor_copy(
                                    vt[:, tt * 512:(tt + 1) * 512], bank(2 + j)),
                                    waits=[tk])
                        vcons[k % 2] = tk
                b.barrier()

                tcons = None
                prev_tnb = [None, None]
                prev_tsr = [None, None]
                prev_out = [None, None]
                for side in range(2):
                    arr = qba if side == 0 else kba
                    tabsrc_c = D["csq"] if side == 0 else D["csk"]
                    tabsrc_s = D["ssq"] if side == 0 else D["ssk"]
                    scol = qs8 if side == 0 else ksc
                    for c in range(NCH):
                        wl = [tcons] if tcons else []
                        tlc = b.dma("sp", tabc[0][:],
                                    tabsrc_c[:, c * 512:(c + 1) * 512],
                                    ch="tabc0", waits=wl)
                        tls = b.dma("sp", tabs[0][:],
                                    tabsrc_s[:, c * 512:(c + 1) * 512],
                                    ch="tabs0", waits=wl)
                        tlast = None
                        for p in range(4):
                            u = p % 2
                            sqa_, t3s_, swp_ = sqas2[u], t3s2[u], swp2[u]
                            cc = slice(c * 512, (c + 1) * 512)
                            tsq = b.op("act", lambda E, p=p, cc=cc, arr=arr,
                                       sqa_=sqa_: E.activation(
                                out=sqa_[:], in_=arr[p][:, cc], func=AF.Square),
                                waits=[prev_tnb[u]] if prev_tnb[u] else [])
                            tnb = mm(b, bank(u), st.uvind[:], sqa_[:],
                                     waits=[tsq] + ([prev_tsr[u]] if prev_tsr[u] else []))
                            prev_tnb[u] = tnb
                            tsr = b.op("act", lambda E, u=u, swp_=swp_: E.activation(
                                out=swp_[:], in_=bank(u), func=AF.Sqrt),
                                waits=[tnb] + ([prev_out[u]] if prev_out[u] else []))
                            prev_tsr[u] = tsr
                            b.op("dve", lambda E, swp_=swp_: E.tensor_scalar_max(
                                swp_[:], swp_[:], 1e-12), waits=[tsr])
                            trc = b.op("dve", lambda E, swp_=swp_: E.reciprocal(
                                swp_[:], swp_[:]))
                            tt3 = b.op("dve", lambda E, p=p, cc=cc, arr=arr,
                                       scol=scol, t3s_=t3s_, swp_=swp_:
                                       E.scalar_tensor_tensor(
                                t3s_[:], arr[p][:, cc], scol[:], swp_[:],
                                ALU.mult, ALU.mult))
                            b.op("dve", lambda E, t3s_=t3s_, swp_=swp_: E.tensor_copy(
                                swp_[0:64, :], t3s_[64:128, :]))
                            b.op("dve", lambda E, t3s_=t3s_, swp_=swp_: E.tensor_copy(
                                swp_[64:128, :], t3s_[0:64, :]))
                            tp1 = b.op("dve", lambda E, t3s_=t3s_: E.tensor_mul(
                                t3s_[:], t3s_[:], tabc[0][:]), waits=[tlc])
                            tp2 = b.op("dve", lambda E, swp_=swp_: E.tensor_mul(
                                swp_[:], swp_[:], tabs[0][:]), waits=[tls])
                            tout = b.op("dve", lambda E, p=p, cc=cc, arr=arr,
                                        t3s_=t3s_, swp_=swp_: E.tensor_add(
                                arr[p][:, cc], t3s_[:], swp_[:]))
                            prev_out[u] = tout
                            tlast = tout
                        tcons = tlast
                b.barrier()
                rot_scope.close()
                pbf2 = [ea(nc.sbuf_tensor(_u(f"pbfD{j}"), [128, 640], BF16)) for j in range(4)]
                pts = [ea(nc.sbuf_tensor(_u(f"ptsD{j}"), [128, 640], BF16)) for j in range(2)]
                if DEBUG_D[0] and l == 0:
                    for p_ in range(4):
                        b.dma("g", D["dbg_q"][p_], qba[p_][:])
                        b.dma("g", D["dbg_k"][p_], kba[p_][:])
                    b.dma("g", D["dbg_v"][:, 0:NQT * 512], vt[:, 0:NQT * 512])
                    b.barrier()

                prev_sc = [None, None]
                prev_pt = [None, None]
                prev_o = [None, None]
                pbf_rd = [None] * 4
                pend = [None]

                def flush_tav():
                    if pend[0] is None:
                        return
                    p, qt, kb0, nkb, tnrm2 = pend[0]
                    pend[0] = None
                    par = qt % 2
                    tevh = [None, None]
                    for hb in (0, 1):
                        tsl = hb
                        ttr = None
                        for i in range(nkb):
                            tw = [tnrm2[hb]]
                            if i == 0 and prev_pt[tsl] is not None:
                                tw.append(prev_pt[tsl])
                            ttr = transp(b, psb[tsl][:, i * 128:(i + 1) * 128],
                                         pbf2[2 * hb + par][:, i * 128:(i + 1) * 128],
                                         st.idb[:], waits=tw,
                                         inc=(i == nkb - 1))
                        pbf_rd[2 * hb + par] = ttr
                        if hb == 0:
                            tev = b.op("act", lambda E, tsl=tsl, nkb=nkb: E.activation(
                                out=pts[tsl][:, 0:nkb * 128],
                                in_=psb[tsl][:, 0:nkb * 128],
                                func=AF.Copy), waits=[ttr])
                        else:
                            tev = b.op("dve", lambda E, tsl=tsl, nkb=nkb: E.tensor_copy(
                                pts[tsl][:, 0:nkb * 128],
                                psb[tsl][:, 0:nkb * 128]), waits=[ttr])
                        prev_pt[tsl] = tev
                        tevh[hb] = tev
                    for hb in (0, 1):
                        h = 2 * p + hb
                        tsl = hb
                        avmm = None
                        for i in range(nkb):
                            kb = kb0 + i
                            avw = [tevh[hb]]
                            if i == 0 and prev_o[hb] is not None:
                                avw.append(prev_o[hb])
                            avmm = mm(b, ps[0:64, (4 + hb) * 512:(4 + hb) * 512 + 128],
                                      vt[:, kb * 512 + h * 64:kb * 512 + h * 64 + 64],
                                      pts[tsl][:, i * 128:(i + 1) * 128],
                                      waits=avw,
                                      start=(i == 0), stop=(i == nkb - 1),
                                      inc=(i == nkb - 1), tp=(0, 0))
                        tevo = b.op("dve", lambda E, p=p, hb=hb, qt=qt: E.tensor_copy(
                            obf[p][64 * hb:64 * hb + 64, qt * 128:(qt + 1) * 128],
                            ps[0:64, (4 + hb) * 512:(4 + hb) * 512 + 128]),
                            waits=[avmm])
                        prev_o[hb] = tevo

                for p in range(4):
                    for qt in range(NQT):
                        kb0 = max(0, qt - 4)
                        nkb = qt - kb0 + 1
                        keys = nkb * 128
                        smm = [None, None]
                        for hb in (0, 1):
                            rows_u = slice(32 * hb, 32 * hb + 32)
                            rows_v = slice(64 + 32 * hb, 64 + 32 * hb + 32)
                            sb0 = 2 * hb
                            w0 = [prev_sc[hb]] if prev_sc[hb] else []
                            for k0 in range(0, keys, 512):
                                k1 = min(k0 + 512, keys)
                                kcols = slice(kb0 * 128 + k0, kb0 * 128 + k1)
                                obank = ps[:, sb0 * 512 + k0: sb0 * 512 + k1]
                                has_band = (qt >= 4 and k0 == 0)
                                has_caus = (k1 == keys)
                                tu = mm(b, obank,
                                        qba[p][rows_u, qt * 128:(qt + 1) * 128],
                                        kba[p][rows_u, kcols], waits=w0,
                                        start=True, stop=False, inc=True,
                                        tp=(32 * hb, 0))
                                w0 = []
                                tv2 = mm(b, obank,
                                         qba[p][rows_v, qt * 128:(qt + 1) * 128],
                                         kba[p][rows_v, kcols], waits=[tu],
                                         start=False,
                                         stop=not (has_band or has_caus),
                                         inc=True,
                                         tp=(64 + 32 * hb, 0))
                                smm[hb] = tv2
                                if has_band:
                                    smm[hb] = mm(b, ps[:, sb0 * 512:sb0 * 512 + 128],
                                                 st.idb[:], st.ml1f4b[:],
                                                 waits=[smm[hb]], start=False,
                                                 stop=not has_caus, inc=True)
                                if has_caus:
                                    smm[hb] = mm(
                                        b, ps[:, sb0 * 512 + keys - 128:sb0 * 512 + keys],
                                        st.idb[:], st.mcb[:],
                                        waits=[smm[hb]], start=False, stop=True,
                                        inc=True)
                        flush_tav()
                        texp = [None, None]
                        tnrm2 = [None, None]
                        for hb in (0, 1):
                            sb0 = 2 * hb
                            tmax = b.op("dve", lambda E, sb0=sb0, keys=keys, hb=hb:
                                        E.reduce_max(nm[:, hb:hb + 1],
                                                     ps[:, sb0 * 512:sb0 * 512 + keys],
                                                     axis=AX.X, negate=True),
                                        waits=[smm[hb]])
                            par = qt % 2
                            wexp = [tmax]
                            if pbf_rd[2 * hb + par] is not None:
                                wexp.append(pbf_rd[2 * hb + par])
                            texp[hb] = b.op("act", lambda E, sb0=sb0, keys=keys,
                                            hb=hb, par=par: E.activation(
                                out=pbf2[2 * hb + par][:, 0:keys],
                                in_=ps[:, sb0 * 512:sb0 * 512 + keys],
                                func=AF.Exp, bias=nm[:, hb:hb + 1],
                                accum_out=rs[:, hb:hb + 1]), waits=wexp)
                            prev_sc[hb] = texp[hb]
                            b.op("dve", lambda E, hb=hb: E.reciprocal(
                                rr[:, hb:hb + 1], rs[:, hb:hb + 1]), waits=[texp[hb]])
                            tnrm2[hb] = b.op("dve", lambda E, keys=keys, hb=hb,
                                             par=par: E.tensor_scalar_mul(
                                pbf2[2 * hb + par][:, 0:keys],
                                pbf2[2 * hb + par][:, 0:keys],
                                rr[:, hb:hb + 1]))
                        pend[0] = (p, qt, kb0, nkb, tnrm2)
                flush_tav()
                b.barrier()
                if DEBUG_D[0] and l == 0:
                    for p_ in range(4):
                        b.dma("g", D["dbg_o"][p_], obf[p_][:])
                    b.barrier()
                sp_.close()

                with (nc.psum_tensor(_u("psD4"), [128, 4096], F32) as ps4,
                      nc.sbuf_tensor(_u("woxD0"), [128, 1024], BF16) as wox0,
                      nc.sbuf_tensor(_u("woxD1"), [128, 1024], BF16) as wox1):
                    wox = [wox0, wox1]
                    bank_rd = [None] * 8
                    kcons = [None, None]
                    for c in range(NCH):
                        dmm = [None] * 8
                        for k in range(4):
                            wk_ = [kcons[k % 2]] if kcons[k % 2] else []
                            two = b.dma("g", wox[k % 2][:],
                                        D["l1f_wo"][l][k * 128:(k + 1) * 128, :],
                                        ch=f"woxD{k % 2}", waits=wk_)
                            for f in range(8):
                                w = [two]
                                if k == 0 and bank_rd[f] is not None:
                                    w.append(bank_rd[f])
                                dmm[f] = mm(b, ps4[:, f * 512:(f + 1) * 512],
                                            wox[k % 2][:, f * 128:(f + 1) * 128],
                                            obf[k][:, c * 512:(c + 1) * 512], waits=w,
                                            start=(k == 0), stop=(k == 3),
                                            inc=(k == 3 or f == 7))
                            kcons[k % 2] = dmm[7]
                        for f in range(8):
                            tres = b.op("dve", lambda E, f=f, c=c: E.tensor_add(
                                st.xf[f][:, c * 512:(c + 1) * 512],
                                st.xf[f][:, c * 512:(c + 1) * 512],
                                ps4[:, f * 512:(f + 1) * 512]),
                                waits=[dmm[f], spill[-1]])
                            bank_rd[f] = tres
                    b.barrier()

            for i in range(8):
                b.op("act", lambda E, i=i: E.activation(
                    out=st.xbf[i][:], in_=st.xf[i][:], func=AF.Copy))
            b.barrier()
            for i in range(4):
                b.op("dve", lambda E, i=i: E.memset(hsh0[i][:, 0:1], 0.0))
                b.op("dve", lambda E, i=i: E.tensor_copy(
                    hsh0[i][:, 1:512], st.xbf[4 + i][:, 0:511]))
            b.barrier()

            with ExitStack() as sf:
                ef = sf.enter_context
                ybf = [ef(nc.sbuf_tensor(_u(f"ybfD{i}"), [128, T], BF16)) for i in range(8)]
                w1a = [ef(nc.sbuf_tensor(_u(f"w1aD{j}"), [128, 8 * 512], BF16)) for j in range(2)]
                w1g = [ef(nc.sbuf_tensor(_u(f"w1gD{j}"), [128, 8 * 512], BF16)) for j in range(2)]
                w2r = [ef(nc.sbuf_tensor(_u(f"w2rD{j}"), [128, 1024], BF16)) for j in range(2)]
                hp = [ef(nc.sbuf_tensor(_u(f"hpD{jp}"), [128, T], BF16)) for jp in range(4)]
                gel = ef(nc.sbuf_tensor(_u("gelD"), [128, 512], F32))
                ps2 = ef(nc.psum_tensor(_u("psD2"), [128, 4096], F32))

                def bank2(i):
                    return ps2[:, i * 512:(i + 1) * 512]

                bank_rd2 = {}
                prev_tt_d = [None]
                last_w2 = [None]
                tgb = []
                for i in range(8):
                    tgb.append(b.dma("sp", lngc[:, i:i + 1],
                                     D["l1f_lng"][l][i * 128:(i + 1) * 128]))
                    tgb.append(b.dma("sp", lnbc[:, i:i + 1],
                                     D["l1f_lnb"][l][i * 128:(i + 1) * 128]))
                    tgb.append(b.dma("sp", b2c[:, i:i + 1],
                                     D["l1f_b2"][l][i * 128:(i + 1) * 128]))
                for i in range(22):
                    tgb.append(b.dma("sp", b1ac[:, i:i + 1],
                                     D["l1f_b1a"][l][i * 128:(i + 1) * 128]))
                    tgb.append(b.dma("sp", b1gc[:, i:i + 1],
                                     D["l1f_b1g"][l][i * 128:(i + 1) * 128]))
                b.wait("dve", tgb)
                b.wait("act", tgb)

                emit_ln(b, st, ps2[:, 0:2048], mode="inner",
                        gb=(lngc, lnbc), ybf=ybf,
                        src_fn=lambda i, c: shifted_bf(i, c * 512, 512),
                        extra=(ps2[:, 2048:2560], ps2[:, 2560:3072]))
                b.barrier()

                w1cons = [None, None]
                w2cons = [None, None]
                cc0 = 0
                for ci, cw in enumerate(CCW):
                    njp = cw // 128
                    slt = ci % 2
                    wl = [w1cons[slt]] if w1cons[slt] else []
                    t1a = b.dma("g", w1a[slt][:, 0:8 * cw].rearrange(
                        "p (k j) -> p k j", k=8),
                        D["l1f_w1"].rearrange("l (k p) j -> l p k j", p=128)
                        [l][:, :, cc0:cc0 + cw], ch=f"w1aD{slt}", waits=wl)
                    t1g = b.dma("g", w1g[slt][:, 0:8 * cw].rearrange(
                        "p (k j) -> p k j", k=8),
                        D["l1f_w1"].rearrange("l (k p) j -> l p k j", p=128)
                        [l][:, :, HP + cc0:HP + cc0 + cw], ch=f"w1gD{slt}", waits=wl)
                    lastw1 = None
                    gu_done = []
                    for jp in range(njp):
                        gi = cc0 // 128 + jp
                        for c in range(NCH):
                            rot = (jp * NCH + c) % 2
                            bA, bB = rot, 2 + rot
                            wA = [t1a]
                            if bank_rd2.get(bA) is not None:
                                wA.append(bank_rd2[bA])
                            amm = None
                            for k in range(8):
                                amm = mm(b, bank2(bA),
                                         w1a[slt][:, k * cw + jp * 128:k * cw + jp * 128 + 128],
                                         ybf[k][:, c * 512:(c + 1) * 512], waits=wA,
                                         start=(k == 0), stop=(k == 7), inc=(k == 7))
                                wA = []
                            wB = [t1g]
                            if bank_rd2.get(bB) is not None:
                                wB.append(bank_rd2[bB])
                            gmm = None
                            for k in range(8):
                                gmm = mm(b, bank2(bB),
                                         w1g[slt][:, k * cw + jp * 128:k * cw + jp * 128 + 128],
                                         ybf[k][:, c * 512:(c + 1) * 512], waits=wB,
                                         start=(k == 0), stop=(k == 7), inc=(k == 7))
                                wB = []
                            lastw1 = gmm
                            tgl = b.op("act", lambda E, bB=bB, gi=gi: E.activation(
                                out=gel[:], in_=bank2(bB), func=AF.Gelu,
                                bias=b1gc[:, gi:gi + 1]),
                                waits=[gmm] + ([prev_tt_d[0]] if prev_tt_d[0] else []))
                            bank_rd2[bB] = tgl
                            th = b.op("dve", lambda E, jp=jp, c=c, bA=bA, gi=gi:
                                      E.scalar_tensor_tensor(
                                hp[jp][:, c * 512:(c + 1) * 512], bank2(bA),
                                b1ac[:, gi:gi + 1], gel[:], ALU.add, ALU.mult),
                                waits=[amm, tgl] + ([last_w2[0]] if last_w2[0] else []))
                            bank_rd2[bA] = th
                            prev_tt_d[0] = th
                            gu_done.append(th)
                    w1cons[slt] = lastw1
                    for c in range(NCH):
                        dmm = None
                        for jp in range(njp):
                            sl2 = jp % 2
                            w2w = [w2cons[sl2]] if w2cons[sl2] else []
                            t2r = b.dma("g", w2r[sl2][:],
                                        D["l1f_w2"][l][cc0 + jp * 128:cc0 + (jp + 1) * 128, :],
                                        ch=f"w2rD{sl2}", waits=w2w)
                            for f in range(8):
                                w = [t2r] + gu_done
                                if jp == 0 and bank_rd2.get(f) is not None:
                                    w.append(bank_rd2[f])
                                dmm = mm(b, bank2(f), w2r[sl2][:, f * 128:(f + 1) * 128],
                                         hp[jp][:, c * 512:(c + 1) * 512], waits=w,
                                         start=(jp == 0), stop=(jp == njp - 1),
                                         inc=(jp == njp - 1 or f == 7))
                            w2cons[sl2] = dmm
                        last_w2[0] = dmm
                        for f in range(8):
                            if ci == len(CCW) - 1:
                                tres = b.op("dve", lambda E, f=f, c=c: E.scalar_tensor_tensor(
                                    st.xf[f][:, c * 512:(c + 1) * 512], bank2(f),
                                    b2c[:, f:f + 1],
                                    st.xf[f][:, c * 512:(c + 1) * 512],
                                    ALU.add, ALU.add), waits=[dmm])
                            else:
                                tres = b.op("dve", lambda E, f=f, c=c: E.tensor_add(
                                    st.xf[f][:, c * 512:(c + 1) * 512],
                                    st.xf[f][:, c * 512:(c + 1) * 512],
                                    bank2(f)), waits=[dmm])
                            bank_rd2[f] = tres
                    cc0 += cw
                b.barrier()
            if l == 0:
                for i in range(8):
                    b.op("act", lambda E, i=i: E.activation(
                        out=st.xbf[i][:], in_=st.xf[i][:], func=AF.Copy))
                b.barrier()

        for l in range(2):
            _run_layer(l)

        with (nc.psum_tensor(_u("psD3"), [128, 3072], F32) as ps3,
              nc.sbuf_tensor(_u("xinD0"), [128, T], F32) as xin0,
              nc.sbuf_tensor(_u("xinD1"), [128, T], F32) as xin1):
            xin = [xin0, xin1]
            tadd = []
            for i in range(8):
                tl = b.dma("sp", xin[i % 2][:], D["xin_spill"][i], ch="spill",
                           waits=(tadd[-1:] if tadd else []) + [spill[-1]])
                ta = b.op("dve", lambda E, i=i: E.tensor_add(
                    st.xf[i][:], st.xf[i][:], xin[i % 2][:]), waits=[tl])
                tadd.append(ta)
            b.barrier()
            emit_ln(b, st, ps3[:, 0:2048], mode="std",
                    extra=(ps3[:, 2048:2560], ps3[:, 2560:3072]))
            b.barrier()



def _run_launch(blocks, xt_cores, W, C, T=2048, NSEQ=4):
    nc, D = build_program(blocks, T, NSEQ)
    names = {k for k in D if k != "yt" and k != "xin_spill"}
    shared = {}
    for k in names:
        if k == "xt":
            continue
        shared[k] = W[k] if k in W else C[k]
    in_maps = []
    for c in range(NCORES):
        m = dict(shared)
        m["xt"] = xt_cores[c]
        in_maps.append(m)
    res = run_bass_kernel_spmd(nc, in_maps, core_ids=list(range(NCORES)))
    return np.stack([np.asarray(r["yt"]) for r in res.results])


def kernel(**inputs):
    T, NSEQ = 2048, 4
    W = pack_weights(inputs)
    C = host_constants(T)
    x = np.asarray(inputs["L0_attn_x"], dtype=np.float32)
    xt = np.ascontiguousarray(x.transpose(0, 2, 1))
    xt_cores = [xt[c * NSEQ:(c + 1) * NSEQ] for c in range(NCORES)]
    yt = _run_launch(["A", "B", "C", "D"], xt_cores, W, C, T, NSEQ)
    y = np.concatenate([yt[c] for c in range(NCORES)], axis=0)
    return np.ascontiguousarray(y.transpose(0, 2, 1))


# version 27
# speedup vs baseline: 1.0202x; 1.0202x over previous
import math
import numpy as np
from contextlib import ExitStack

import concourse.bass as bass
import concourse.mybir as mybir
from concourse.bass_utils import run_bass_kernel_spmd

F32 = mybir.dt.float32
BF16 = mybir.dt.bfloat16
AF = mybir.ActivationFunctionType
ALU = mybir.AluOpType
AX = mybir.AxisListType

MASKV = -30000.0
NCORES = 8


def host_constants(T):
    C = {}
    C["ident"] = np.eye(128, dtype=np.float32)
    r = np.arange(128)
    C["mcaus"] = np.where(r[:, None] >= r[None, :], 0.0, MASKV).astype(np.float32)
    C["m_l1a2"] = np.where(r[:, None] < r[None, :], 0.0, MASKV).astype(np.float32)
    C["m_l1f4"] = np.where(r[:, None] <= r[None, :], 0.0, MASKV).astype(np.float32)
    NCH = T // 512
    E = np.zeros((128, NCH * NCH), dtype=np.float32)
    for c in range(NCH):
        E[:, c * NCH + c] = 1.0
    C["esum"] = E
    selp = np.zeros((NCH, NCH * 128), dtype=np.float32)
    seln = np.zeros((NCH, NCH * 128), dtype=np.float32)
    for c in range(NCH):
        selp[c, c * 128:(c + 1) * 128] = 1.0
        seln[c, c * 128:(c + 1) * 128] = -1.0
    C["selp"], C["seln"] = selp, seln
    import ml_dtypes
    s32 = np.zeros((32, 32 * 128), dtype=np.float32)
    for e in range(32):
        s32[e, e * 128:(e + 1) * 128] = 1.0
    C["sel32"] = s32
    C["sel32b"] = s32.astype(ml_dtypes.bfloat16)
    head = (r // 32) % 2
    C["uvind"] = (head[:, None] == head[None, :]).astype(np.float32)
    DH, WIN = 64, 512
    d = np.arange(0, DH, 2, dtype=np.float64)
    inv = 1.0 / (10000.0 ** (d / DH))
    t = np.arange(T, dtype=np.float64)
    fr = t[:, None] * inv[None, :]
    base = (d + 0.4 * DH) / (1.4 * DH)
    power = (t - T // 2) / float(WIN // 2)
    sc = base[None, :] ** power[:, None]
    cos, sin = np.cos(fr), np.sin(fr)
    def tile4(a):
        a = a.T.astype(np.float32)
        return np.concatenate([a, a, a, a], axis=0)
    sgn = np.where(r[:, None] < 64, -1.0, 1.0).astype(np.float32)
    C["csq"] = tile4(cos * sc)
    C["ssq"] = sgn * tile4(sin * sc)
    C["csk"] = tile4(cos / sc)
    C["ssk"] = sgn * tile4(sin / sc)
    return C


def pack_weights(inp):
    W = {}
    W["l0_wq"] = inp["L0_attn_Wq"]
    W["l0_wk"] = inp["L0_attn_Wk"]
    W["l0_wvT"] = inp["L0_attn_Wv"].T
    W["l0_woT"] = inp["L0_attn_Wo"].T
    W["gwT"] = inp["L0_ffn_gate_w"].T
    W["guwT"] = inp["L0_ffn_gate_up_w"].transpose(0, 2, 1)
    W["dwT"] = inp["L0_ffn_down_w"].transpose(0, 2, 1)
    W["l1a_wqkv"] = inp["L1_attn_W_qkv"]
    W["l1a_bqkv"] = inp["L1_attn_b_qkv"]
    W["l1a_wout"] = inp["L1_attn_W_out"]
    W["l1a_bout"] = inp["L1_attn_b_out"]
    perm = []
    for p in range(4):
        for (h, d0) in ((2 * p, 0), (2 * p + 1, 0), (2 * p, 32), (2 * p + 1, 32)):
            perm += [h * 64 + d0 + j for j in range(32)]
    perm = np.array(perm)
    wq = inp["L1_ffn_wqkv"]
    W["l1f_wqkv"] = np.concatenate(
        [wq[:, :, perm], wq[:, :, 512 + perm], wq[:, :, 1024:]], axis=2)
    qs, ks = inp["L1_ffn_q_scale"], inp["L1_ffn_k_scale"]
    W["l1f_qscol"] = np.concatenate(
        [qs[:, :32], qs[:, :32], qs[:, 32:], qs[:, 32:]], axis=1)
    W["l1f_kscol"] = np.concatenate(
        [ks[:, :32], ks[:, :32], ks[:, 32:], ks[:, 32:]], axis=1)
    W["l1f_wo"] = inp["L1_ffn_wo"]
    W["l1f_lng"] = inp["L1_ffn_ln_g"]
    W["l1f_lnb"] = inp["L1_ffn_ln_b"]
    HID, HP = 2730, 2816
    w1 = inp["L1_ffn_w1"]
    w1p = np.zeros((2, 1024, 2 * HP), dtype=np.float32)
    w1p[:, :, :HID] = w1[:, :, :HID]
    w1p[:, :, HP:HP + HID] = w1[:, :, HID:]
    W["l1f_w1"] = w1p
    b1 = inp["L1_ffn_b1"]
    b1a = np.zeros((2, HP), dtype=np.float32); b1a[:, :HID] = b1[:, :HID]
    b1g = np.zeros((2, HP), dtype=np.float32); b1g[:, :HID] = b1[:, HID:]
    W["l1f_b1a"], W["l1f_b1g"] = b1a, b1g
    w2 = inp["L1_ffn_w2"]
    w2p = np.zeros((2, HP, 1024), dtype=np.float32)
    w2p[:, :HID] = w2
    W["l1f_w2"] = w2p
    W["l1f_b2"] = inp["L1_ffn_b2"]
    return {k: np.ascontiguousarray(np.asarray(v, dtype=np.float32)) for k, v in W.items()}



ENGS = ("sp", "g", "pe", "dve", "act")

class BB:
    def __init__(self, nc, stack):
        self.nc = nc
        self._stack = stack
        self.q = {e: [] for e in ENGS}
        self.sem = {}
        for name in ("S_sp", "S_g", "S_pe", "S_dve", "S_act", "D_sp", "D_g"):
            self.sem[name] = stack.enter_context(nc.semaphore(name))
        self.cnt = {name: 0 for name in self.sem}
        self.esem = {"sp": "S_sp", "g": "S_g", "pe": "S_pe",
                     "dve": "S_dve", "act": "S_act"}
        self.auto = {"dve", "act"}
        self.bar_toks = []
        self.waited = {e: {} for e in ENGS}

    def _filt(self, eng, wl):
        seen = self.waited[eng]
        out = []
        for (n, v) in wl:
            if seen.get(n, 0) < v:
                seen[n] = v
                out.append((n, v))
        return tuple(out)

    def tok(self, eng):
        s = self.esem[eng]
        return (s, self.cnt[s])

    def dtok(self, eng):
        s = "D_" + eng
        return (s, self.cnt[s])

    def _coalesce(self, waits):
        m = {}
        for (n, v) in waits:
            if n.startswith("D_"):
                v = self.cnt[n]
            if v > 0 and v > m.get(n, 0):
                m[n] = v
        return tuple(m.items())

    def op(self, eng, fn, waits=(), inc=True):
        s = self.esem[eng]
        wl = list(waits)
        if eng in self.auto:
            if self.cnt[s] > 0:
                wl.append((s, self.cnt[s]))
            inc = True
        wl = self._filt(eng, self._coalesce(wl))
        if inc:
            self.cnt[s] += 1
        val = self.cnt[s]
        sems = self.sem

        def run(E, fn=fn, wl=wl, inc=inc, s=s):
            for (n, v) in wl:
                E.wait_ge(sems[n], v)
            r = fn(E)
            if inc:
                r.then_inc(sems[s], 1)
        self.q[eng].append(run)
        return (s, val)

    def chan(self, name):
        key = "D_" + name
        if key not in self.sem:
            self.sem[key] = self._stack.enter_context(self.nc.semaphore(key))
            self.cnt[key] = 0
        return key

    def dma(self, eng, out, in_, waits=(), with_bar=True, ch=None):
        ds = "D_" + eng if ch is None else self.chan(ch)
        wl = list(waits)
        if with_bar:
            wl += self.bar_toks
        wl = self._filt(eng, self._coalesce(wl))
        self.cnt[ds] += 16
        val = self.cnt[ds]
        sems = self.sem

        def run(E, out=out, in_=in_, wl=wl, ds=ds):
            for (n, v) in wl:
                E.wait_ge(sems[n], v)
            E.dma_start(out=out, in_=in_).then_inc(sems[ds], 16)
        self.q[eng].append(run)
        return (ds, val)

    def wait(self, eng, waits):
        wl = self._filt(eng, self._coalesce(waits))
        sems = self.sem

        def run(E, wl=wl):
            for (n, v) in wl:
                E.wait_ge(sems[n], v)
        self.q[eng].append(run)

    def barrier(self):
        toks = []
        for e in ENGS:
            if e in ("pe", "dve", "act"):
                toks.append(self.op(e, lambda E: E.drain()))
            else:
                toks.append(self.op(e, lambda E: E.nop()))
        toks += [self.dtok("sp"), self.dtok("g")]
        toks += [(n, self.cnt[n]) for n in self.cnt if n.startswith("D_xb")]
        for e in ENGS:
            self.wait(e, toks)
        self.bar_toks = toks
        return toks

    def flush(self, block):
        decs = {"sp": block.sync, "g": block.gpsimd, "pe": block.tensor,
                "dve": block.vector, "act": block.scalar}
        for e in ENGS:
            ops = self.q[e]
            def body(E, ops=ops):
                for f in ops:
                    f(E)
            decs[e](body)


def mm(b, out, lhsT, rhs, waits=(), start=True, stop=True, inc=True, tp=None):
    def fn(E, out=out, lhsT=lhsT, rhs=rhs, start=start, stop=stop, tp=tp):
        kw = {}
        if tp is not None:
            kw["tile_position"] = tp
        return E.matmul(out, lhsT, rhs, start=start, stop=stop,
                        skip_group_check=True, **kw)
    return b.op("pe", fn, waits=waits, inc=inc)


def transp(b, out, in_, ident, waits=(), inc=True):
    return b.op("pe", lambda E: E.transpose(out, in_, ident), waits=waits, inc=inc)


class S:
    pass


_UC = [0]
DEBUG_D = [False]

def _u(name):
    _UC[0] += 1
    return f"{name}_{_UC[0]}"



def declare_dram(nc, blocks, T, NSEQ):
    D = {}
    def di(name, shape):
        D[name] = nc.dram_tensor(name, list(shape), F32, kind="ExternalInput").ap()
    di("xt", (NSEQ, 1024, T))
    D["yt"] = nc.dram_tensor("yt", [NSEQ, 1024, T], F32, kind="ExternalOutput").ap()
    di("ident", (128, 128))
    di("mcaus", (128, 128))
    NCH = T // 512
    di("esum", (128, NCH * NCH))
    di("selp", (NCH, NCH * 128))
    di("seln", (NCH, NCH * 128))
    if "A" in blocks:
        di("l0_wq", (64, 64)); di("l0_wk", (64, 64))
        di("l0_wvT", (64, 64)); di("l0_woT", (1024, 1024))
    if "B" in blocks:
        di("gwT", (1024, 32))
        D["sel32b"] = nc.dram_tensor("sel32b", [32, 32 * 128], BF16,
                                     kind="ExternalInput").ap()
        di("guwT", (32, 1024, 1024)); di("dwT", (32, 512, 1024))
    if "C" in blocks:
        di("m_l1a2", (128, 128))
        di("l1a_wqkv", (1024, 3072)); di("l1a_bqkv", (3072,))
        di("l1a_wout", (1024, 1024)); di("l1a_bout", (1024,))
    if "D" in blocks:
        di("m_l1f4", (128, 128)); di("uvind", (128, 128))
        di("csq", (128, T)); di("ssq", (128, T))
        di("csk", (128, T)); di("ssk", (128, T))
        di("l1f_wqkv", (2, 1024, 1536))
        di("l1f_qscol", (2, 128)); di("l1f_kscol", (2, 128))
        di("l1f_wo", (2, 512, 1024))
        di("l1f_lng", (2, 1024)); di("l1f_lnb", (2, 1024))
        di("l1f_w1", (2, 1024, 2 * 2816))
        di("l1f_b1a", (2, 2816)); di("l1f_b1g", (2, 2816))
        di("l1f_w2", (2, 2816, 1024)); di("l1f_b2", (2, 1024))
        D["xin_spill"] = nc.dram_tensor(
            "xin_spill", [8, 128, T], F32, kind="Internal").ap()
        if DEBUG_D[0]:
            D["dbg_q"] = nc.dram_tensor("dbg_q", [4, 128, T], F32,
                                        kind="ExternalOutput").ap()
            D["dbg_k"] = nc.dram_tensor("dbg_k", [4, 128, T], F32,
                                        kind="ExternalOutput").ap()
            D["dbg_o"] = nc.dram_tensor("dbg_o", [4, 128, T], F32,
                                        kind="ExternalOutput").ap()
            D["dbg_v"] = nc.dram_tensor("dbg_v", [128, 16 * 512], F32,
                                        kind="ExternalOutput").ap()
    return D


def alloc_persistent(b, stack, D, T, blocks):
    nc = b.nc
    st = S()
    st.T = T
    st.NCH = T // 512
    st.NQT = T // 128
    sb = lambda name, shape, dt: stack.enter_context(
        nc.sbuf_tensor(name, list(shape), dt))
    st.sb = sb
    st.xf = [sb(f"xf{i}", [128, T], F32) for i in range(8)]
    st.xbf = [sb(f"xbf{i}", [128, T], BF16) for i in range(8)]
    st.idf = sb("idf", [128, 128], F32)
    st.idb = sb("idb", [128, 128], BF16)
    st.mcaus = sb("mcausS", [128, 128], F32)
    st.esum = sb("esumS", [128, st.NCH * st.NCH], BF16)
    st.selp = sb("selpS", [st.NCH, st.NCH * 128], F32)
    st.seln = sb("selnS", [st.NCH, st.NCH * 128], F32)
    st.xnb = sb("xnb", [128, 512], BF16)
    st.xnb2 = sb("xnb2", [128, 512], BF16)
    st.sqa = sb("sqa", [128, 512], BF16)
    st.sqa2 = sb("sqa2", [128, 512], BF16)
    st.stat = sb("stat", [st.NCH, 512], F32)
    st.stat2 = sb("stat2", [st.NCH, 512], F32)
    st.stat3 = sb("stat3", [st.NCH, 512], F32)
    st.scr512 = sb("scr512", [128, 512], F32)
    st.eps = sb("eps", [128, 1], F32)
    st.mcb = sb("mcausB", [128, 128], BF16)
    if "C" in blocks:
        st.m_l1a2 = sb("ml1a2S", [128, 128], F32)
        st.ml1a2b = sb("ml1a2B", [128, 128], BF16)
    if "D" in blocks:
        st.m_l1f4 = sb("ml1f4S", [128, 128], F32)
        st.ml1f4b = sb("ml1f4B", [128, 128], BF16)
        st.uvind = sb("uvindS", [128, 128], BF16)
    teps = b.op("dve", lambda E: E.memset(st.eps[:], 1e-5))
    tl = [teps]
    tl.append(b.dma("sp", st.idf[:], D["ident"][:]))
    tl.append(b.dma("g", st.idb[:], D["ident"][:]))
    tl.append(b.dma("sp", st.mcaus[:], D["mcaus"][:]))
    tl.append(b.dma("g", st.esum[:], D["esum"][:]))
    tl.append(b.dma("sp", st.selp[:], D["selp"][:]))
    tl.append(b.dma("sp", st.seln[:], D["seln"][:]))
    tl.append(b.dma("g", st.mcb[:], D["mcaus"][:]))
    if "C" in blocks:
        tl.append(b.dma("sp", st.m_l1a2[:], D["m_l1a2"][:]))
        tl.append(b.dma("g", st.ml1a2b[:], D["m_l1a2"][:]))
    if "D" in blocks:
        tl.append(b.dma("sp", st.m_l1f4[:], D["m_l1f4"][:]))
        tl.append(b.dma("g", st.ml1f4b[:], D["m_l1f4"][:]))
        tl.append(b.dma("g", st.uvind[:], D["uvind"][:]))
    st.const_toks = tl
    return st



def emit_ln(b, st, ps, mode="std", gb=None, ybf=None, src_fn=None, extra=None):
    NCH = st.NCH
    psA, psB = ps[:, 0:512], ps[:, 512:1024]
    psMs = [ps[:, 1024:1536]]
    psRs = [ps[:, 1536:2048]]
    if extra is not None:
        psMs.append(extra[0])
        psRs.append(extra[1])
    else:
        psMs.append(psMs[0])
        psRs.append(psRs[0])
    t2 = None
    prev_mmB = [None, None]
    xnbs = [st.xnb, st.xnb2]
    sqas = [st.sqa, st.sqa2]
    for c in range(NCH):
        for i in range(8):
            par = i % 2
            xnb, sqa = xnbs[par], sqas[par]
            wprev = [prev_mmB[par]] if prev_mmB[par] else []
            if mode == "inner":
                rhs_sum = src_fn(i, c)
                tc = None
            else:
                tc = b.op("act", lambda E, i=i, c=c, xnb=xnb: E.activation(
                    out=xnb[:], in_=st.xf[i][:, c * 512:(c + 1) * 512],
                    func=AF.Copy), waits=wprev)
                rhs_sum = xnb[:]
            tsq = b.op("act", lambda E, r=rhs_sum, sqa=sqa: E.activation(
                out=sqa[:], in_=r, func=AF.Square), waits=wprev)
            w1 = [tc] if tc else []
            mm(b, psA[0:NCH, :], st.esum[:, c * NCH:(c + 1) * NCH], rhs_sum,
               waits=w1, start=(i == 0 and c == 0), stop=(i == 7 and c == NCH - 1),
               inc=False)
            t2 = mm(b, psB[0:NCH, :], st.esum[:, c * NCH:(c + 1) * NCH], sqa[:],
                    waits=[tsq], start=(i == 0 and c == 0),
                    stop=(i == 7 and c == NCH - 1), inc=True)
            prev_mmB[par] = t2
    c1 = 1.0 / 1024.0
    tm = b.op("dve", lambda E: E.tensor_scalar_mul(st.stat[0:NCH, :], psA[0:NCH, :], c1),
              waits=[t2])
    te = b.op("dve", lambda E: E.tensor_scalar_mul(st.stat3[0:NCH, :], psB[0:NCH, :], c1))
    tq = b.op("dve", lambda E: E.tensor_mul(st.stat2[0:NCH, :], st.stat[0:NCH, :],
                                            st.stat[0:NCH, :]))
    tv = b.op("dve", lambda E: E.tensor_sub(st.stat3[0:NCH, :], st.stat3[0:NCH, :],
                                            st.stat2[0:NCH, :]))
    ts = b.op("act", lambda E: E.activation(out=st.stat2[0:NCH, :],
                                            in_=st.stat3[0:NCH, :], func=AF.Sqrt,
                                            bias=st.eps[0:NCH, :]), waits=[tv])
    tr = b.op("dve", lambda E: E.reciprocal(st.stat2[0:NCH, :], st.stat2[0:NCH, :]),
              waits=[ts])
    out_toks = []
    last_apply = [None, None]
    for c in range(NCH):
        par = c % 2
        psM, psR = psMs[par], psRs[par]
        wb = [tm, tr] + ([last_apply[par]] if last_apply[par] else [])
        tbm = mm(b, psM, st.seln[0:NCH, c * 128:(c + 1) * 128], st.stat[0:NCH, :],
                 waits=wb)
        tbr = mm(b, psR, st.selp[0:NCH, c * 128:(c + 1) * 128], st.stat2[0:NCH, :],
                 waits=[tbm])
        for i in range(8):
            if mode == "inner":
                ta = b.op("dve", lambda E, s_=src_fn(i, c), psM=psM: E.tensor_add(
                    st.scr512[:], s_, psM), waits=[tbm])
                tb = b.op("dve", lambda E, psR=psR: E.tensor_mul(
                    st.scr512[:], st.scr512[:], psR), waits=[tbr])
                g, bb_ = gb
                tg = b.op("dve", lambda E, i=i, c=c, g=g, bb_=bb_: E.tensor_scalar(
                    ybf[i][:, c * 512:(c + 1) * 512], st.scr512[:],
                    g[:, i:i + 1], bb_[:, i:i + 1], ALU.mult, ALU.add))
                out_toks.append(tg)
                last_apply[par] = tg
            else:
                ta = b.op("dve", lambda E, i=i, c=c, psM=psM: E.tensor_add(
                    st.xf[i][:, c * 512:(c + 1) * 512],
                    st.xf[i][:, c * 512:(c + 1) * 512], psM), waits=[tbm])
                tb = b.op("dve", lambda E, i=i, c=c, psR=psR: E.tensor_mul(
                    st.xf[i][:, c * 512:(c + 1) * 512],
                    st.xf[i][:, c * 512:(c + 1) * 512], psR), waits=[tbr])
                tcst = b.op("act", lambda E, i=i, c=c: E.activation(
                    out=st.xbf[i][:, c * 512:(c + 1) * 512],
                    in_=st.xf[i][:, c * 512:(c + 1) * 512], func=AF.Copy),
                    waits=[tb])
                out_toks.append(tcst)
                last_apply[par] = tb
    return out_toks



def emit_block_A(b, st, D, stack, s):
    nc = b.nc
    T, NCH, NQT = st.T, st.NCH, st.NQT
    with ExitStack() as sc:
        e = sc.enter_context
        a64 = e(nc.sbuf_tensor(_u("a64"), [128, 64], BF16))
        wv2 = e(nc.sbuf_tensor(_u("wv2"), [128, 64], BF16))
        wqs = e(nc.sbuf_tensor(_u("wqs"), [64, 64], BF16))
        wks = e(nc.sbuf_tensor(_u("wks"), [64, 64], BF16))
        zbf = [e(nc.sbuf_tensor(_u(f"zbf{j}"), [128, T], BF16)) for j in range(2)]
        vtok = [e(nc.sbuf_tensor(_u(f"vtok{j}"), [128, T], BF16)) for j in range(2)]
        pbfs = [e(nc.sbuf_tensor(_u(f"pbf{j}"), [128, T], BF16)) for j in range(2)]
        pts = [e(nc.sbuf_tensor(_u(f"pts{j}"), [128, 1024], BF16)) for j in range(2)]
        avT = [e(nc.sbuf_tensor(_u(f"avT{i}"), [128, T], BF16)) for i in range(8)]
        woT = e(nc.sbuf_tensor(_u("woT"), [128, 8 * 1024], BF16))
        nm = e(nc.sbuf_tensor(_u("nm"), [128, 1], F32))
        rs4 = e(nc.sbuf_tensor(_u("rs4"), [128, 4], F32))
        rs = e(nc.sbuf_tensor(_u("rs"), [128, 1], F32))
        rr = e(nc.sbuf_tensor(_u("rr"), [128, 1], F32))
        ps = e(nc.psum_tensor(_u("psA"), [128, 2048], F32))
        ps_t = [e(nc.psum_tensor(_u(f"psT{j}"), [128, 1024], BF16)) for j in range(2)]
        ps_o = e(nc.psum_tensor(_u("psO"), [128, 512], F32))
        ps_m = e(nc.psum_tensor(_u("psMZ"), [128, 512], F32))

        t1 = b.dma("g", wqs[:], D["l0_wq"][:])
        t2 = b.dma("g", wks[:], D["l0_wk"][:])
        t3 = b.dma("g", wv2[0:64, :], D["l0_wvT"][:])
        t4 = b.dma("g", wv2[64:128, :], D["l0_wvT"][:])
        twoT = b.dma("g", woT[:].rearrange("p (k f) -> p k f", k=8),
                     D["l0_woT"].rearrange("(k p) f -> p k f", p=128),
                     ch="woT")
        if getattr(st, "issue_xbf", None) is not None:
            st.xbf_tok = st.issue_xbf()
            st.xbf_ready = list(st.xbf_tok)
            st.issue_xbf = None
        tA = mm(b, ps_m[0:64, 0:64], wqs[:], wks[:], waits=[t1, t2])
        tA1 = b.op("dve", lambda E: E.tensor_scalar_mul(
            a64[0:64, :], ps_m[0:64, 0:64], 0.125), waits=[tA])
        tA2 = b.op("dve", lambda E: E.tensor_scalar_mul(
            a64[64:128, :], ps_m[0:64, 0:64], 0.125))

        pass
        bank_exp = [None] * 4
        prev_pt_tok = [None, None]
        pendA = [None]

        def flush_tavA():
            state = pendA[0]
            if state is None:
                return
            p_, hb_, qt_, tnrm_, vt_ = state
            pendA[0] = None
            pbf = pbfs[qt_ % 2]
            avmm = None
            nkb = qt_ + 1
            bats = [(s, min(s + 4, nkb)) for s in range(0, nkb, 4)]
            tevs = []
            for bi, (b0, b1) in enumerate(bats):
                tsl = bi % 2
                ttr = None
                for i in range(b0, b1):
                    tw = [tnrm_]
                    if i == b0 and prev_pt_tok[tsl] is not None:
                        tw.append(prev_pt_tok[tsl])
                    ttr = transp(b, ps_t[tsl][:, (i - b0) * 128:(i - b0 + 1) * 128],
                                 pbf[:, i * 128:(i + 1) * 128], st.idb[:],
                                 waits=tw, inc=(i == b1 - 1))
                    pbf_rdA[qt_ % 2] = ttr
                nb = (b1 - b0) * 128
                if (bi % 2) == 0:
                    tev = b.op("act", lambda E, tsl=tsl, nb=nb: E.activation(
                        out=pts[tsl][:, 0:nb], in_=ps_t[tsl][:, 0:nb],
                        func=AF.Copy), waits=[ttr])
                else:
                    tev = b.op("dve", lambda E, tsl=tsl, nb=nb: E.tensor_copy(
                        pts[tsl][:, 0:nb], ps_t[tsl][:, 0:nb]), waits=[ttr])
                prev_pt_tok[tsl] = tev
                tevs.append(tev)
            for bi, (b0, b1) in enumerate(bats):
                tsl = bi % 2
                for i in range(b0, b1):
                    avw = [tevs[bi]]
                    if i == 0 and prev_o_tok_l[0] is not None:
                        avw.append(prev_o_tok_l[0])
                    avmm = mm(b, ps_o[0:64, 0:128],
                              vt_[:, i * 128 + 64 * hb_: i * 128 + 64 * hb_ + 64],
                              pts[tsl][:, (i - b0) * 128:(i - b0 + 1) * 128],
                              waits=avw,
                              start=(i == 0), stop=(i == qt_),
                              inc=(i == qt_))
            tev = b.op("dve", lambda E, hb_=hb_, qt_=qt_, p_=p_: E.tensor_copy(
                avT[p_][64 * hb_:64 * hb_ + 64, qt_ * 128:(qt_ + 1) * 128],
                ps_o[0:64, 0:128]), waits=[avmm])
            prev_o_tok_l[0] = tev

        prev_o_tok_l = [None]
        pbf_rdA = [None, None]
        for p in range(8):
            sl = p % 2
            zb, vt = zbf[sl], vtok[sl]
            zw = [st.xbf_tok[p], tA1, tA2, t3, t4]
            if prev_o_tok_l[0] is not None:
                zw.append(prev_o_tok_l[0])
            last_ev = None
            for c in range(NCH):
                for hb in (0, 1):
                    t = mm(b, ps_m[0:64, :], a64[64 * hb:64 * hb + 64, :],
                           st.xbf[p][64 * hb:64 * hb + 64, c * 512:(c + 1) * 512],
                           waits=zw, tp=(64 * hb, 0))
                    tz = b.op("dve", lambda E, hb=hb, c=c, zb=zb: E.tensor_copy(
                        zb[64 * hb:64 * hb + 64, c * 512:(c + 1) * 512],
                        ps_m[0:64, :]), waits=[t])
                    zw = [tz]
                    last_ev = tz
            for tt in range(NQT):
                for hb in (0, 1):
                    t = mm(b, ps_m[:, 0:64],
                           st.xbf[p][64 * hb:64 * hb + 64, tt * 128:(tt + 1) * 128],
                           wv2[64 * hb:64 * hb + 64, :], waits=zw, tp=(64 * hb, 0))
                    tv = b.op("dve", lambda E, tt=tt, hb=hb, vt=vt: E.tensor_copy(
                        vt[:, tt * 128 + 64 * hb: tt * 128 + 64 * hb + 64],
                        ps_m[:, 0:64]), waits=[t])
                    zw = [tv]
                    last_ev = tv
            zv_done = [last_ev]

            for hb in (0, 1):
                h_rows = slice(64 * hb, 64 * hb + 64)
                for qt in range(NQT):
                    keys = (qt + 1) * 128
                    ncb = (keys + 511) // 512
                    cd = ncb - 1
                    order = [cd] + list(range(cd))
                    s_tok = {}
                    w0 = list(zv_done)
                    for c in order:
                        k0 = c * 512
                        k1 = min(k0 + 512, keys)
                        w = w0 + ([bank_exp[c]] if bank_exp[c] else [])
                        w0 = []
                        s_tok[c] = mm(b, ps[:, k0:k1],
                                      zb[h_rows, qt * 128:(qt + 1) * 128],
                                      st.xbf[p][h_rows, k0:k1], waits=w,
                                      tp=(64 * hb, 0),
                                      start=True, stop=(c != cd), inc=True)
                        if c == cd:
                            s_tok[c] = mm(b, ps[:, qt * 128:(qt + 1) * 128],
                                          st.idb[:], st.mcb[:],
                                          waits=[s_tok[c]],
                                          start=False, stop=True, inc=True)
                    flush_tavA()
                    tmax = b.op("dve", lambda E, qt=qt: E.reduce_max(
                        nm[:], ps[:, qt * 128:(qt + 1) * 128], axis=AX.X,
                        negate=True), waits=[s_tok[cd]])
                    texp = {}
                    pbf = pbfs[qt % 2]
                    for c in order:
                        k0 = c * 512
                        k1 = min(k0 + 512, keys)
                        we = [tmax, s_tok[c]]
                        if pbf_rdA[qt % 2] is not None:
                            we.append(pbf_rdA[qt % 2])
                        texp[c] = b.op("act", lambda E, k0=k0, k1=k1, c=c, pbf=pbf:
                                       E.activation(
                            out=pbf[:, k0:k1], in_=ps[:, k0:k1], func=AF.Exp,
                            bias=nm[:], accum_out=rs4[:, c:c + 1]), waits=we)
                        bank_exp[c] = texp[c]
                    tsum = b.op("dve", lambda E, ncb=ncb: E.reduce_sum(
                        rs[:], rs4[:, 0:ncb], axis=AX.X),
                        waits=[texp[c] for c in order])
                    trr = b.op("dve", lambda E: E.reciprocal(rr[:], rs[:]))
                    tnrm = b.op("dve", lambda E, keys=keys, pbf=pbf: E.tensor_scalar_mul(
                        pbf[:, 0:keys], pbf[:, 0:keys], rr[:]))
                    pendA[0] = (p, hb, qt, tnrm, vt)


        flush_tavA()
        b.barrier()
        bank_rd = [None] * 4
        for c in range(NCH):
            for f in range(8):
                bk = f % 4
                w0 = [twoT] if (f == 0 and c == 0) else []
                if bank_rd[bk] is not None:
                    w0.append(bank_rd[bk])
                dmm = None
                for k in range(8):
                    dmm = mm(b, ps[:, bk * 512:(bk + 1) * 512],
                             woT[:, k * 1024 + f * 128: k * 1024 + f * 128 + 128],
                             avT[k][:, c * 512:(c + 1) * 512],
                             waits=w0, start=(k == 0), stop=(k == 7), inc=(k == 7))
                    w0 = []
                tres = b.op("dve", lambda E, f=f, c=c, bk=bk: E.tensor_add(
                    st.xf[f][:, c * 512:(c + 1) * 512],
                    st.xf[f][:, c * 512:(c + 1) * 512],
                    ps[:, bk * 512:(bk + 1) * 512]), waits=[dmm])
                bank_rd[bk] = tres
        b.barrier()
        emit_ln(b, st, ps, mode="std", extra=(ps_m[:, 0:512], ps_o[:, 0:512]))
        b.barrier()



def build_program(blocks, T=2048, NSEQ=4):
    nc = bass.Bass("TRN2", target_bir_lowering=False, debug=False)
    stack = ExitStack()
    with stack:
        D = declare_dram(nc, blocks, T, NSEQ)
        b = BB(nc, stack)
        blk = stack.enter_context(nc.Block())
        st = alloc_persistent(b, stack, D, T, blocks)
        for s in range(NSEQ):
            lt = []
            for i in range(8):
                lt.append(b.dma("sp", st.xf[i][:],
                                D["xt"][s, i * 128:(i + 1) * 128, :]))

            def issue_xbf(s=s):
                return [b.dma("g", st.xbf[i][:],
                              D["xt"][s, i * 128:(i + 1) * 128, :],
                              ch=f"xb{i}") for i in range(8)]
            if s == 0:
                bt = issue_xbf()
                st.issue_xbf = None
                b.wait("pe", lt + bt + st.const_toks)
                b.barrier()
                st.xbf_tok = [b.bar_toks[0]] * 8
                st.xbf_ready = [b.bar_toks[0]]
            else:
                st.issue_xbf = issue_xbf
            if "A" in blocks:
                emit_block_A(b, st, D, stack, s)
            if "B" in blocks:
                emit_block_B(b, st, D, stack, s)
            if "C" in blocks:
                emit_block_C(b, st, D, stack, s)
            if "D" in blocks:
                emit_block_D(b, st, D, stack, s)
            sto = []
            for i in range(8):
                sto.append(b.dma("sp", D["yt"][s, i * 128:(i + 1) * 128, :],
                                 st.xf[i][:]))
            b.wait("sp", sto)
            b.barrier()
        b.flush(blk)
    return nc, D


def emit_block_B(b, st, D, stack, s):
    nc = b.nc
    T, NCH, NQT = st.T, st.NCH, st.NQT
    CP = min(1024, T)
    NPASS = T // CP
    NSUB = CP // 512
    GRP = 4
    with ExitStack() as sc:
        e = sc.enter_context
        gwTs = e(nc.sbuf_tensor(_u("gwTs"), [128, 8 * 32], BF16))
        lgs = e(nc.sbuf_tensor(_u("lgs"), [32, 512], F32))
        probs = e(nc.sbuf_tensor(_u("probs"), [128, 32], F32))
        max8 = e(nc.sbuf_tensor(_u("max8"), [128, 8], F32))
        resid = e(nc.sbuf_tensor(_u("resid"), [128, 32], F32))
        combf = e(nc.sbuf_tensor(_u("combf"), [128, 32], F32))
        combT = e(nc.sbuf_tensor(_u("combT"), [32, T], BF16))
        selb = [e(nc.sbuf_tensor(_u(f"selb{j}"), [32, 128], BF16)) for j in range(2)]
        nm = e(nc.sbuf_tensor(_u("nmB"), [128, 1], F32))
        rs = e(nc.sbuf_tensor(_u("rsB"), [128, 1], F32))
        rr = e(nc.sbuf_tensor(_u("rrB"), [128, 1], F32))
        guw = [e(nc.sbuf_tensor(_u(f"guw{j}"), [128, 8 * 1024], BF16)) for j in range(2)]
        dwt = [e(nc.sbuf_tensor(_u(f"dwt{j}"), [128, 4 * 512], BF16)) for j in range(4)]
        hst = [[e(nc.sbuf_tensor(_u(f"hst{g}_{jp}"), [128, CP], BF16))
                for jp in range(4)] for g in range(GRP)]
        sgb = st.scr512
        tb = e(nc.sbuf_tensor(_u("tbB"), [128, 512], F32))
        cbsb = e(nc.sbuf_tensor(_u("cbsb"), [128, 1024], BF16))
        ps = e(nc.psum_tensor(_u("psB"), [128, 4096], F32))

        def bank(i):
            return ps[:, i * 512:(i + 1) * 512]

        tgw = b.dma("g", gwTs[:].rearrange("p (k e) -> p k e", k=8),
                    D["gwT"].rearrange("(k p) e -> p k e", p=128))
        selcons = [None, None]

        lgev = None
        lgrd = None
        t_b1rd = None
        t_b2rd = None
        for c in range(NCH):
            lmm = None
            for k in range(8):
                lmm = mm(b, ps[0:32, 0:512], gwTs[:, k * 32:(k + 1) * 32],
                         st.xbf[k][:, c * 512:(c + 1) * 512],
                         waits=[tgw] + st.xbf_ready + ([lgev] if lgev else []),
                         start=(k == 0), stop=(k == 7), inc=(k == 7))
            lgev = b.op("dve", lambda E, c=c: E.tensor_copy(
                lgs[0:32, 0:512], ps[0:32, 0:512]),
                waits=[lmm] + ([lgrd] if lgrd else []))
            for tj in range(4):
                tt = c * 4 + tj
                ttr = transp(b, ps[:, 512:544], lgs[0:32, tj * 128:(tj + 1) * 128],
                             st.idf[0:32, 0:32],
                             waits=[lgev] + ([t_b1rd] if t_b1rd else []))
                lgrd = ttr
                tmax = b.op("dve", lambda E: E.reduce_max(nm[:], ps[:, 512:544],
                                                          axis=AX.X, negate=True),
                            waits=[ttr])
                texp = b.op("act", lambda E: E.activation(
                    out=probs[:], in_=ps[:, 512:544], func=AF.Exp, bias=nm[:]),
                    waits=[tmax])
                t_b1rd = texp
                b.op("dve", lambda E: E.max(out=max8[:], in_=probs[:]), waits=[texp])
                b.op("dve", lambda E: E.match_replace(
                    out=resid[:], in_to_replace=max8[:], in_values=probs[:],
                    imm_value=0.0))
                b.op("dve", lambda E: E.tensor_sub(resid[:], probs[:], resid[:]))
                b.op("dve", lambda E: E.reduce_sum(rs[:], resid[:], axis=AX.X))
                b.op("dve", lambda E: E.reciprocal(rr[:], rs[:]))
                tcm = b.op("dve", lambda E: E.tensor_scalar_mul(
                    combf[:], resid[:], rr[:]))
                tct = transp(b, ps[0:32, 1024:1152], combf[:], st.idf[:],
                             waits=[tcm] + ([t_b2rd] if t_b2rd else []))
                tev = b.op("act", lambda E, tt=tt: E.activation(
                    out=combT[0:32, tt * 128:(tt + 1) * 128],
                    in_=ps[0:32, 1024:1152],
                    func=AF.Copy), waits=[tct])
                t_b2rd = tev
        b.barrier()

        cons_g = [None, None]
        load_g = [None, None]
        cons_dw = [None] * 4
        load_dw_tok = [None] * 4
        bank_rd = [None] * 8
        last_down = [None]

        def load_guw(e_, slot):
            w = [cons_g[slot]] if cons_g[slot] else []
            load_g[slot] = b.dma(
                "g", guw[slot][:].rearrange("p (k j) -> p k j", k=8),
                D["guwT"][e_].rearrange("(k p) j -> p k j", p=128),
                waits=w, ch=f"guw{slot}")

        def load_dw(e_, half):
            sl = e_ % 4
            w = [cons_dw[sl]] if cons_dw[sl] else []
            load_dw_tok[sl] = b.dma(
                "g", dwt[sl][:].rearrange("p (jp f) -> p jp f", jp=4),
                D["dwT"][e_].rearrange("(jp p) f -> p jp f", p=128)
                [:, :, half * 512:(half + 1) * 512],
                waits=w, ch=f"dwt{sl}")

        first_loads = [False]
        for cp in range(NPASS):
            t0 = cp * CP
            for g0 in range(0, 32, GRP):
                if not first_loads[0]:
                    load_guw(g0 + 0, 0)
                    load_guw(g0 + 1, 1)
                    first_loads[0] = True
                gu_done = []
                for ei in range(GRP):
                    e_ = g0 + ei
                    slot = ei % 2
                    ssl = e_ % 2
                    tsel = b.dma("sp", selb[ssl][:],
                                 D["sel32b"][:, e_ * 128:(e_ + 1) * 128],
                                 waits=[selcons[ssl]] if selcons[ssl] else [],
                                 ch=f"selb{ssl}")
                    cbt = [None] * NSUB
                    for sub in range(NSUB):
                        cbk = sub
                        w = [tsel, t_b2rd]
                        if bank_rd[cbk] is not None:
                            w.append(bank_rd[cbk])
                        tcbm = mm(b, bank(cbk),
                                  selb[ssl][:],
                                  combT[0:32, t0 + sub * 512: t0 + (sub + 1) * 512],
                                  waits=w)
                        selcons[ssl] = tcbm
                        cbt[sub] = b.op("dve", lambda E, sub=sub, cbk=cbk: E.tensor_copy(
                            cbsb[:, sub * 512:(sub + 1) * 512], bank(cbk)),
                            waits=[tcbm])
                        bank_rd[cbk] = cbt[sub]
                    lastpe = None
                    prev_tt = getattr(emit_block_B, "_ptt", None)
                    for jp in range(4):
                        for sub in range(NSUB):
                            rot = (jp * NSUB + sub) % 2
                            bA, bB = rot, 2 + rot
                            tokc = slice(t0 + sub * 512, t0 + (sub + 1) * 512)
                            wA = [load_g[slot]]
                            if bank_rd[bA] is not None:
                                wA.append(bank_rd[bA])
                            gmm = None
                            for k in range(8):
                                gmm = mm(b, bank(bA),
                                         guw[slot][:, k * 1024 + jp * 128:
                                                   k * 1024 + jp * 128 + 128],
                                         st.xbf[k][:, tokc], waits=wA,
                                         start=(k == 0), stop=(k == 7),
                                         inc=(k == 7))
                                wA = []
                            wB = []
                            if bank_rd[bB] is not None:
                                wB.append(bank_rd[bB])
                            umm = None
                            for k in range(8):
                                umm = mm(b, bank(bB),
                                         guw[slot][:, k * 1024 + 512 + jp * 128:
                                                   k * 1024 + 512 + jp * 128 + 128],
                                         st.xbf[k][:, tokc], waits=wB,
                                         start=(k == 0), stop=(k == 7),
                                         inc=(k == 7))
                                wB = []
                            lastpe = umm
                            tsg = b.op("act", lambda E, bA=bA: E.activation(
                                out=sgb[:], in_=bank(bA), func=AF.Silu),
                                waits=[gmm] + ([prev_tt] if prev_tt else []))
                            bank_rd[bA] = tsg
                            tt_ = b.op("dve", lambda E, bB=bB: E.tensor_mul(
                                tb[:], bank(bB), sgb[:]), waits=[umm, tsg])
                            bank_rd[bB] = tt_
                            prev_tt = tt_
                            wth = [cbt[sub]]
                            if last_down[0] is not None:
                                wth.append(last_down[0])
                            th = b.op("dve", lambda E, ei=ei, jp=jp, sub=sub:
                                      E.tensor_mul(
                                hst[ei][jp][:, sub * 512:(sub + 1) * 512],
                                tb[:], cbsb[:, sub * 512:(sub + 1) * 512]),
                                waits=wth)
                            gu_done.append(th)
                    emit_block_B._ptt = prev_tt
                    cons_g[slot] = lastpe
                    load_dw(e_, 0)
                    if ei == 1:
                        load_guw(g0 + 2, 0)
                        load_guw(g0 + 3, 1)
                ng = (g0 + GRP) % 32
                load_guw(ng + 0, 0)
                load_guw(ng + 1, 1)
                sidx = 0
                nsweep = 2 * NSUB
                for fh in range(2):
                    for q in range(2):
                        for sub in range(NSUB):
                            swp_i = q * NSUB + sub
                            par = sidx % 2
                            sidx += 1
                            b4 = 4 + 2 * par
                            sweep_end = None
                            for ei in range(GRP):
                                sl = (g0 + ei) % 4
                                ldw = load_dw_tok[sl]
                                if fh == 0 and swp_i == nsweep - 1 and q + sub > 0:
                                    pass
                                for jp in range(4):
                                    for ft in range(2):
                                        bk = b4 + ft
                                        w = [ldw] + gu_done
                                        if (ei == 0 and jp == 0 and
                                                bank_rd[bk] is not None):
                                            w.append(bank_rd[bk])
                                        last = (ei == GRP - 1 and jp == 3)
                                        sweep_end = mm(
                                            b, bank(bk),
                                            dwt[sl][:, jp * 512 + (2 * q + ft) * 128:
                                                    jp * 512 + (2 * q + ft + 1) * 128],
                                            hst[ei][jp][:, sub * 512:(sub + 1) * 512],
                                            waits=w,
                                            start=(ei == 0 and jp == 0),
                                            stop=last,
                                            inc=(ft == 1 and jp == 3))
                                cons_dw[sl] = sweep_end
                            last_down[0] = sweep_end
                            if fh == 0 and swp_i == nsweep - 1:
                                for ei2 in range(GRP):
                                    load_dw(g0 + ei2, 1)
                            for ft in range(2):
                                fi = 4 * fh + 2 * q + ft
                                bk = b4 + ft
                                tcol = slice(t0 + sub * 512, t0 + (sub + 1) * 512)
                                tres = b.op("dve", lambda E, fi=fi, bk=bk, tcol=tcol:
                                            E.tensor_add(
                                    st.xf[fi][:, tcol], st.xf[fi][:, tcol],
                                    bank(bk)), waits=[sweep_end])
                                bank_rd[bk] = tres
        b.barrier()
        emit_ln(b, st, ps[:, 0:2048], mode="std",
                extra=(ps[:, 2048:2560], ps[:, 2560:3072]))
        b.barrier()


def emit_block_C(b, st, D, stack, s):
    nc = b.nc
    T, NCH, NQT = st.T, st.NCH, st.NQT
    with ExitStack() as sc:
        e = sc.enter_context
        wvr = [e(nc.sbuf_tensor(_u(f"wvr{j}"), [128, 512], BF16)) for j in range(2)]
        wor = [e(nc.sbuf_tensor(_u(f"wor{j}"), [128, 1024], BF16)) for j in range(2)]
        wqp = [e(nc.sbuf_tensor(_u(f"wqp{j}"), [128, 8 * 128], BF16)) for j in range(2)]
        wkp = [e(nc.sbuf_tensor(_u(f"wkp{j}"), [128, 8 * 128], BF16)) for j in range(2)]
        bq = e(nc.sbuf_tensor(_u("bq"), [128, 8], F32))
        bk = e(nc.sbuf_tensor(_u("bk"), [128, 8], F32))
        bv = e(nc.sbuf_tensor(_u("bv"), [128, 8], F32))
        bo = e(nc.sbuf_tensor(_u("bo"), [128, 8], F32))
        vt = e(nc.sbuf_tensor(_u("vt"), [128, NQT * 1024], BF16))
        qbf = e(nc.sbuf_tensor(_u("qbf"), [128, T], BF16))
        kbf = e(nc.sbuf_tensor(_u("kbf"), [128, T], BF16))
        avT = [e(nc.sbuf_tensor(_u(f"avTC{i}"), [128, T], BF16)) for i in range(8)]
        pbf2 = [e(nc.sbuf_tensor(_u(f"pbfC{j}"), [128, 384], BF16)) for j in range(4)]
        pts = [e(nc.sbuf_tensor(_u(f"ptsC{j}"), [128, 384], BF16)) for j in range(2)]
        nm = e(nc.sbuf_tensor(_u("nmC"), [128, 2], F32))
        rs = e(nc.sbuf_tensor(_u("rsC"), [128, 2], F32))
        rr = e(nc.sbuf_tensor(_u("rrC"), [128, 2], F32))

        tb_ = []
        for p in range(8):
            tb_.append(b.dma("sp", bq[:, p:p + 1], D["l1a_bqkv"][p * 128:(p + 1) * 128]))
            tb_.append(b.dma("sp", bk[:, p:p + 1],
                             D["l1a_bqkv"][1024 + p * 128:1024 + (p + 1) * 128]))
            tb_.append(b.dma("sp", bv[:, p:p + 1],
                             D["l1a_bqkv"][2048 + p * 128:2048 + (p + 1) * 128]))
            tb_.append(b.dma("sp", bo[:, p:p + 1], D["l1a_bout"][p * 128:(p + 1) * 128]))

        with (nc.psum_tensor(_u("psC"), [128, 3072], F32) as ps,
              nc.psum_tensor(_u("psbC0"), [128, 1024], BF16) as psb0,
              nc.psum_tensor(_u("psbC1"), [128, 1024], BF16) as psb1):
            psb = [psb0, psb1]

            def bank(i):
                return ps[:, i * 512:(i + 1) * 512]

            vev = [None] * 2
            vcons = [None] * 2
            for tg in range(NQT // 2):
                vmm = [None] * 2
                for k in range(8):
                    tk = None
                    for hf in range(2):
                        sl4 = (2 * k + hf) % 2
                        tw = b.dma("g", wvr[sl4][:],
                                   D["l1a_wqkv"][k * 128:(k + 1) * 128,
                                                 2048 + hf * 512:2048 + (hf + 1) * 512],
                                   ch=f"wvr{sl4}",
                                   waits=[vcons[sl4]] if vcons[sl4] else [])
                        for j in range(2):
                            tt = tg * 2 + j
                            w = [tw] + st.xbf_ready
                            if k == 0 and hf == 0 and vev[j] is not None:
                                w.append(vev[j])
                            tk = mm(b, bank(2 * j + hf),
                                    st.xbf[k][:, tt * 128:(tt + 1) * 128],
                                    wvr[sl4][:],
                                    waits=w, start=(k == 0), stop=(k == 7),
                                    inc=(j == 1 or k == 7))
                            w = []
                            if j == 1 or k == 7:
                                vmm[j] = tk
                        vcons[sl4] = tk
                for j in range(2):
                    tt = tg * 2 + j
                    vev[j] = b.op("dve", lambda E, tt=tt, j=j: E.tensor_copy(
                        vt[:, tt * 1024:(tt + 1) * 1024],
                        ps[:, 2 * j * 512: 2 * j * 512 + 1024]), waits=[vmm[j]])
            b.barrier()

            prev_q = [None, None]
            prev_sc = [None, None]
            prev_pt = [None, None]
            prev_o = [None, None]
            prev_pair_o = None
            pair_cons = [None, None]
            pbf_rd = [None] * 4
            sc_rd = [None] * 4
            pend = [None]

            def flush_tav():
                if pend[0] is None:
                    return
                p, qt, kb0, nkb, tnrm = pend[0]
                pend[0] = None
                par = qt % 2
                tevh = [None, None]
                for hb in (0, 1):
                    tsl = hb
                    ttr = None
                    for i in range(nkb):
                        tw = [tnrm[hb]]
                        if i == 0 and prev_pt[tsl] is not None:
                            tw.append(prev_pt[tsl])
                        ttr = transp(b, psb[tsl][:, i * 128:(i + 1) * 128],
                                     pbf2[2 * hb + par][:, i * 128:(i + 1) * 128],
                                     st.idb[:], waits=tw, inc=(i == nkb - 1))
                    pbf_rd[2 * hb + par] = ttr
                    if hb == 0:
                        tev = b.op("act", lambda E, tsl=tsl, nkb=nkb: E.activation(
                            out=pts[tsl][:, 0:nkb * 128],
                            in_=psb[tsl][:, 0:nkb * 128], func=AF.Copy),
                            waits=[ttr])
                    else:
                        tev = b.op("dve", lambda E, tsl=tsl, nkb=nkb: E.tensor_copy(
                            pts[tsl][:, 0:nkb * 128],
                            psb[tsl][:, 0:nkb * 128]), waits=[ttr])
                    prev_pt[tsl] = tev
                    tevh[hb] = tev
                for hb in (0, 1):
                    h = 2 * p + hb
                    tsl = hb
                    avmm = None
                    for i in range(nkb):
                        kb = kb0 + i
                        avw = [tevh[hb]]
                        if i == 0 and prev_o[hb] is not None:
                            avw.append(prev_o[hb])
                        avmm = mm(b, bank(4 + hb)[0:64, 0:128],
                                  vt[:, kb * 1024 + h * 64:kb * 1024 + h * 64 + 64],
                                  pts[tsl][:, i * 128:(i + 1) * 128],
                                  waits=avw,
                                  start=(i == 0), stop=(i == nkb - 1),
                                  inc=(i == nkb - 1), tp=(0, 0))
                    tevo = b.op("dve", lambda E, hb=hb, qt=qt, p=p: E.tensor_scalar_add(
                        avT[p][64 * hb:64 * hb + 64, qt * 128:(qt + 1) * 128],
                        bank(4 + hb)[0:64, 0:128],
                        bv[64 * hb:64 * hb + 64, p:p + 1]), waits=[avmm])
                    prev_o[hb] = tevo
            for p in range(8):
                sl = p % 2
                wc = [pair_cons[sl]] if pair_cons[sl] else []
                twq = b.dma("g", wqp[sl][:].rearrange("p (k c) -> p k c", k=8),
                            D["l1a_wqkv"].rearrange("(k p) c -> p k c", p=128)
                            [:, :, p * 128:(p + 1) * 128],
                            ch=f"wqp{sl}", waits=wc)
                twk = b.dma("g", wkp[sl][:].rearrange("p (k c) -> p k c", k=8),
                            D["l1a_wqkv"].rearrange("(k p) c -> p k c", p=128)
                            [:, :, 1024 + p * 128:1024 + (p + 1) * 128],
                            ch=f"wkp{sl}", waits=wc)
                qev = None
                kev = None
                for c in range(NCH):
                    w = [twq] + ([prev_q[0]] if prev_q[0] else [])
                    if sc_rd[1] is not None:
                        w.append(sc_rd[1])
                    qmm = None
                    for k in range(8):
                        qmm = mm(b, bank(0), wqp[sl][:, k * 128:(k + 1) * 128],
                                 st.xbf[k][:, c * 512:(c + 1) * 512], waits=w,
                                 start=(k == 0), stop=(k == 7), inc=(k == 7))
                        w = []
                    wq2 = [qmm] + ([prev_pair_o] if prev_pair_o else [])
                    qev = b.op("dve", lambda E, c=c, p=p: E.tensor_scalar(
                        qbf[:, c * 512:(c + 1) * 512], bank(0), bq[:, p:p + 1],
                        0.125, ALU.add, ALU.mult), waits=wq2)
                    prev_q[0] = qev
                    w = [twk] + ([prev_q[1]] if prev_q[1] else [])
                    if sc_rd[3] is not None:
                        w.append(sc_rd[3])
                    kmm = None
                    for k in range(8):
                        kmm = mm(b, bank(1), wkp[sl][:, k * 128:(k + 1) * 128],
                                 st.xbf[k][:, c * 512:(c + 1) * 512], waits=w,
                                 start=(k == 0), stop=(k == 7), inc=(k == 7))
                        w = []
                    wk2 = [kmm] + ([prev_pair_o] if prev_pair_o else [])
                    kev = b.op("dve", lambda E, c=c, p=p: E.tensor_scalar_add(
                        kbf[:, c * 512:(c + 1) * 512], bank(1), bk[:, p:p + 1]),
                        waits=wk2)
                    prev_q[1] = kev
                qk_done = [qev, kev]

                for qt in range(NQT):
                    kb0 = max(0, qt - 2)
                    nkb = qt - kb0 + 1
                    keys = nkb * 128
                    parq = qt % 2
                    smm = [None, None]
                    for hb in (0, 1):
                        hr = slice(64 * hb, 64 * hb + 64)
                        sbk = (2 + hb) if parq == 0 else hb
                        has_band = (qt >= 2)
                        w = qk_done + ([sc_rd[2 * hb + parq]] if sc_rd[2 * hb + parq] else [])
                        tS = mm(b, bank(sbk)[:, 0:keys],
                                qbf[hr, qt * 128:(qt + 1) * 128],
                                kbf[hr, kb0 * 128:(qt + 1) * 128],
                                waits=w, tp=(64 * hb, 0),
                                start=True, stop=False, inc=True)
                        if has_band:
                            tS = mm(b, bank(sbk)[:, 0:128], st.idb[:],
                                    st.ml1a2b[:], waits=[tS],
                                    start=False, stop=False, inc=True)
                        smm[hb] = mm(b, bank(sbk)[:, keys - 128:keys], st.idb[:],
                                     st.mcb[:], waits=[tS],
                                     start=False, stop=True, inc=True)
                    flush_tav()
                    texp = [None, None]
                    tnrm = [None, None]
                    for hb in (0, 1):
                        sbk = (2 + hb) if parq == 0 else hb
                        tmax = b.op("dve", lambda E, sbk=sbk, keys=keys, hb=hb:
                                    E.reduce_max(nm[:, hb:hb + 1],
                                                 bank(sbk)[:, 0:keys],
                                                 axis=AX.X, negate=True),
                                    waits=[smm[hb]])
                        par = qt % 2
                        wexp = [tmax]
                        if pbf_rd[2 * hb + par] is not None:
                            wexp.append(pbf_rd[2 * hb + par])
                        texp[hb] = b.op("act", lambda E, sbk=sbk, keys=keys, hb=hb,
                                        par=par: E.activation(
                            out=pbf2[2 * hb + par][:, 0:keys],
                            in_=bank(sbk)[:, 0:keys],
                            func=AF.Exp, bias=nm[:, hb:hb + 1],
                            accum_out=rs[:, hb:hb + 1]), waits=wexp)
                        sc_rd[2 * hb + par] = texp[hb]
                        b.op("dve", lambda E, hb=hb: E.reciprocal(
                            rr[:, hb:hb + 1], rs[:, hb:hb + 1]), waits=[texp[hb]])
                        tnrm[hb] = b.op("dve", lambda E, keys=keys, hb=hb,
                                        par=par: E.tensor_scalar_mul(
                            pbf2[2 * hb + par][:, 0:keys],
                            pbf2[2 * hb + par][:, 0:keys],
                            rr[:, hb:hb + 1]))
                    pend[0] = (p, qt, kb0, nkb, tnrm)
                flush_tav()
                pair_cons[sl] = prev_o[1]
                prev_pair_o = prev_o[1]
            b.barrier()

        with nc.psum_tensor(_u("psC2"), [128, 4096], F32) as ps2:
            def bank2(i):
                return ps2[:, i * 512:(i + 1) * 512]
            bank_rd = [None] * 8
            kcons = [None, None]
            for c in range(NCH):
                dmm = [None] * 8
                for k in range(8):
                    wk_ = [kcons[k % 2]] if kcons[k % 2] else []
                    two = b.dma("g", wor[k % 2][:],
                                D["l1a_wout"][k * 128:(k + 1) * 128, :],
                                ch=f"wor{k % 2}", waits=wk_)
                    for f in range(8):
                        w = [two]
                        if k == 0 and bank_rd[f] is not None:
                            w.append(bank_rd[f])
                        dmm[f] = mm(b, bank2(f), wor[k % 2][:, f * 128:(f + 1) * 128],
                                    avT[k][:, c * 512:(c + 1) * 512], waits=w,
                                    start=(k == 0), stop=(k == 7),
                                    inc=(k == 7 or f == 7))
                    kcons[k % 2] = dmm[7]
                for f in range(8):
                    tres = b.op("dve", lambda E, f=f, c=c: E.scalar_tensor_tensor(
                        st.xf[f][:, c * 512:(c + 1) * 512], bank2(f), bo[:, f:f + 1],
                        st.xf[f][:, c * 512:(c + 1) * 512], ALU.add, ALU.add),
                        waits=[dmm[f]])
                    bank_rd[f] = tres
            b.barrier()
            emit_ln(b, st, ps2[:, 0:2048], mode="std",
                    extra=(ps2[:, 2048:2560], ps2[:, 2560:3072]))
            b.barrier()




def emit_block_D(b, st, D, stack, s):
    nc = b.nc
    T, NCH, NQT = st.T, st.NCH, st.NQT
    HP = 2816
    CCW = [512] * 5 + [256]
    with ExitStack() as sc:
        e = sc.enter_context
        hsh0 = [e(nc.sbuf_tensor(_u(f"hsh0_{i}"), [128, 512], BF16)) for i in range(4)]
        qsc = e(nc.sbuf_tensor(_u("qsc"), [128, 1], F32))
        qs8 = e(nc.sbuf_tensor(_u("qs8"), [128, 1], F32))
        ksc = e(nc.sbuf_tensor(_u("ksc"), [128, 1], F32))
        lngc = e(nc.sbuf_tensor(_u("lngc"), [128, 8], F32))
        lnbc = e(nc.sbuf_tensor(_u("lnbc"), [128, 8], F32))
        b1ac = e(nc.sbuf_tensor(_u("b1ac"), [128, 22], F32))
        b1gc = e(nc.sbuf_tensor(_u("b1gc"), [128, 22], F32))
        b2c = e(nc.sbuf_tensor(_u("b2c"), [128, 8], F32))

        spill = []
        for i in range(8):
            spill.append(b.dma("sp", D["xin_spill"][i], st.xf[i][:], ch="spill"))

        def shifted_bf(i, c0, w):
            if i < 4:
                return st.xbf[i][:, c0:c0 + w]
            if c0 == 0:
                return hsh0[i - 4][:, 0:w]
            return st.xbf[i][:, c0 - 1:c0 + w - 1]

        def _run_layer(l):
            hsh_t = []
            for i in range(4):
                b.op("dve", lambda E, i=i: E.memset(hsh0[i][:, 0:1], 0.0))
                t2 = b.op("dve", lambda E, i=i: E.tensor_copy(
                    hsh0[i][:, 1:512], st.xbf[4 + i][:, 0:511]))
                hsh_t.append(t2)
            tqs = b.dma("sp", qsc[:], D["l1f_qscol"][l])
            tks = b.dma("sp", ksc[:], D["l1f_kscol"][l])
            tqs8 = b.op("dve", lambda E: E.tensor_scalar_mul(qs8[:], qsc[:], 8.0),
                        waits=[tqs])

            with ExitStack() as sa:
                ea = sa.enter_context
                wqp = [ea(nc.sbuf_tensor(_u(f"wqpD{j}"), [128, 8 * 128], BF16)) for j in range(2)]
                wkp = [ea(nc.sbuf_tensor(_u(f"wkpD{j}"), [128, 8 * 128], BF16)) for j in range(2)]
                wvv = [ea(nc.sbuf_tensor(_u(f"wvvD{j}"), [128, 512], BF16)) for j in range(2)]
                qba = [ea(nc.sbuf_tensor(_u(f"qbaD{p}"), [128, T], BF16)) for p in range(4)]
                kba = [ea(nc.sbuf_tensor(_u(f"kbaD{p}"), [128, T], BF16)) for p in range(4)]
                vt = ea(nc.sbuf_tensor(_u("vtD"), [128, NQT * 512], BF16))
                obf = [ea(nc.sbuf_tensor(_u(f"obfD{i}"), [128, T], BF16)) for i in range(4)]

                nm = ea(nc.sbuf_tensor(_u("nmD"), [128, 2], F32))
                rs = ea(nc.sbuf_tensor(_u("rsD"), [128, 2], F32))
                rr = ea(nc.sbuf_tensor(_u("rrD"), [128, 2], F32))
                rot_scope = ExitStack()
                t3s2 = [rot_scope.enter_context(nc.sbuf_tensor(_u(f"t3sD{j}"), [128, 512], F32))
                        for j in range(2)]
                swp2 = [rot_scope.enter_context(nc.sbuf_tensor(_u(f"swpD{j}"), [128, 512], F32))
                        for j in range(2)]
                tabc = [rot_scope.enter_context(nc.sbuf_tensor(_u("tabcD0"), [128, 512], F32))]
                tabs = [rot_scope.enter_context(nc.sbuf_tensor(_u("tabsD0"), [128, 512], F32))]
                sqas2 = [st.sqa, st.sqa2]
                sp_ = ExitStack()
                ps = sp_.enter_context(nc.psum_tensor(_u("psD"), [128, 3072], F32))
                psb = [sp_.enter_context(nc.psum_tensor(_u(f"psbD{j}"), [128, 1024], BF16))
                       for j in range(2)]

                def bank(i):
                    return ps[:, i * 512:(i + 1) * 512]

                qkcons = [None, None]
                qev = kev = None
                qmm = kmm = None
                for p in range(4):
                    sl = p % 2
                    wc = [qkcons[sl]] if qkcons[sl] else []
                    twq = b.dma("g", wqp[sl][:].rearrange("p (k c) -> p k c", k=8),
                                D["l1f_wqkv"].rearrange("l (k p) c -> l p k c", p=128)
                                [l][:, :, p * 128:(p + 1) * 128], ch=f"wqpD{sl}",
                                waits=wc)
                    twk = b.dma("g", wkp[sl][:].rearrange("p (k c) -> p k c", k=8),
                                D["l1f_wqkv"].rearrange("l (k p) c -> l p k c", p=128)
                                [l][:, :, 512 + p * 128:512 + (p + 1) * 128],
                                ch=f"wkpD{sl}", waits=wc)
                    for c in range(NCH):
                        w = [twq] + st.xbf_ready + hsh_t + ([qev] if qev else [])
                        qmm = None
                        for k in range(8):
                            qmm = mm(b, bank(0), wqp[sl][:, k * 128:(k + 1) * 128],
                                     shifted_bf(k, c * 512, 512), waits=w,
                                     start=(k == 0), stop=(k == 7), inc=(k == 7))
                            w = []
                        qev = b.op("dve", lambda E, p=p, c=c: E.tensor_copy(
                            qba[p][:, c * 512:(c + 1) * 512], bank(0)), waits=[qmm])
                        w = [twk] + ([kev] if kev else [])
                        kmm = None
                        for k in range(8):
                            kmm = mm(b, bank(1), wkp[sl][:, k * 128:(k + 1) * 128],
                                     shifted_bf(k, c * 512, 512), waits=w,
                                     start=(k == 0), stop=(k == 7), inc=(k == 7))
                            w = []
                        kev = b.op("dve", lambda E, p=p, c=c: E.tensor_copy(
                            kba[p][:, c * 512:(c + 1) * 512], bank(1)), waits=[kmm])
                    qkcons[sl] = kmm
                vev = [None] * 4
                vcons = [None, None]
                for tg in range(NQT // 4):
                    tk = None
                    for k in range(8):
                        tw = b.dma("g", wvv[k % 2][:],
                                   D["l1f_wqkv"][l][k * 128:(k + 1) * 128, 1024:1536],
                                   ch=f"wvvD{k % 2}",
                                   waits=[vcons[k % 2]] if vcons[k % 2] else [])
                        for j in range(4):
                            tt = tg * 4 + j
                            w = [tw] + hsh_t
                            if k == 0 and vev[j] is not None:
                                w.append(vev[j])
                            if k < 4:
                                lhs = st.xbf[k][:, tt * 128:(tt + 1) * 128]
                            elif tt == 0:
                                lhs = hsh0[k - 4][:, 0:128]
                            else:
                                lhs = st.xbf[k][:, tt * 128 - 1:(tt + 1) * 128 - 1]
                            tk = mm(b, bank(2 + j), lhs, wvv[k % 2][:],
                                    waits=w, start=(k == 0), stop=(k == 7),
                                    inc=(j == 3 or k == 7))
                            if k == 7:
                                vev[j] = b.op("dve", lambda E, tt=tt, j=j: E.tensor_copy(
                                    vt[:, tt * 512:(tt + 1) * 512], bank(2 + j)),
                                    waits=[tk])
                        vcons[k % 2] = tk
                b.barrier()

                tcons = None
                prev_tnb = [None, None]
                prev_tsr = [None, None]
                prev_out = [None, None]
                for side in range(2):
                    arr = qba if side == 0 else kba
                    tabsrc_c = D["csq"] if side == 0 else D["csk"]
                    tabsrc_s = D["ssq"] if side == 0 else D["ssk"]
                    scol = qs8 if side == 0 else ksc
                    for c in range(NCH):
                        wl = [tcons] if tcons else []
                        tlc = b.dma("sp", tabc[0][:],
                                    tabsrc_c[:, c * 512:(c + 1) * 512],
                                    ch="tabc0", waits=wl)
                        tls = b.dma("sp", tabs[0][:],
                                    tabsrc_s[:, c * 512:(c + 1) * 512],
                                    ch="tabs0", waits=wl)
                        tlast = None
                        for p in range(4):
                            u = p % 2
                            sqa_, t3s_, swp_ = sqas2[u], t3s2[u], swp2[u]
                            cc = slice(c * 512, (c + 1) * 512)
                            tsq = b.op("act", lambda E, p=p, cc=cc, arr=arr,
                                       sqa_=sqa_: E.activation(
                                out=sqa_[:], in_=arr[p][:, cc], func=AF.Square),
                                waits=[prev_tnb[u]] if prev_tnb[u] else [])
                            tnb = mm(b, bank(u), st.uvind[:], sqa_[:],
                                     waits=[tsq] + ([prev_tsr[u]] if prev_tsr[u] else []))
                            prev_tnb[u] = tnb
                            tsr = b.op("act", lambda E, u=u, swp_=swp_: E.activation(
                                out=swp_[:], in_=bank(u), func=AF.Sqrt),
                                waits=[tnb] + ([prev_out[u]] if prev_out[u] else []))
                            prev_tsr[u] = tsr
                            b.op("dve", lambda E, swp_=swp_: E.tensor_scalar_max(
                                swp_[:], swp_[:], 1e-12), waits=[tsr])
                            trc = b.op("dve", lambda E, swp_=swp_: E.reciprocal(
                                swp_[:], swp_[:]))
                            tt3 = b.op("dve", lambda E, p=p, cc=cc, arr=arr,
                                       scol=scol, t3s_=t3s_, swp_=swp_:
                                       E.scalar_tensor_tensor(
                                t3s_[:], arr[p][:, cc], scol[:], swp_[:],
                                ALU.mult, ALU.mult))
                            b.op("dve", lambda E, t3s_=t3s_, swp_=swp_: E.tensor_copy(
                                swp_[0:64, :], t3s_[64:128, :]))
                            b.op("dve", lambda E, t3s_=t3s_, swp_=swp_: E.tensor_copy(
                                swp_[64:128, :], t3s_[0:64, :]))
                            tp1 = b.op("dve", lambda E, t3s_=t3s_: E.tensor_mul(
                                t3s_[:], t3s_[:], tabc[0][:]), waits=[tlc])
                            tp2 = b.op("dve", lambda E, swp_=swp_: E.tensor_mul(
                                swp_[:], swp_[:], tabs[0][:]), waits=[tls])
                            tout = b.op("dve", lambda E, p=p, cc=cc, arr=arr,
                                        t3s_=t3s_, swp_=swp_: E.tensor_add(
                                arr[p][:, cc], t3s_[:], swp_[:]))
                            prev_out[u] = tout
                            tlast = tout
                        tcons = tlast
                b.barrier()
                rot_scope.close()
                pbf2 = [ea(nc.sbuf_tensor(_u(f"pbfD{j}"), [128, 640], BF16)) for j in range(4)]
                pts = [ea(nc.sbuf_tensor(_u(f"ptsD{j}"), [128, 640], BF16)) for j in range(2)]
                if DEBUG_D[0] and l == 0:
                    for p_ in range(4):
                        b.dma("g", D["dbg_q"][p_], qba[p_][:])
                        b.dma("g", D["dbg_k"][p_], kba[p_][:])
                    b.dma("g", D["dbg_v"][:, 0:NQT * 512], vt[:, 0:NQT * 512])
                    b.barrier()

                prev_sc = [None, None]
                prev_pt = [None, None]
                prev_o = [None, None]
                pbf_rd = [None] * 4
                pend = [None]

                def flush_tav():
                    if pend[0] is None:
                        return
                    p, qt, kb0, nkb, tnrm2 = pend[0]
                    pend[0] = None
                    par = qt % 2
                    tevh = [None, None]
                    for hb in (0, 1):
                        tsl = hb
                        ttr = None
                        for i in range(nkb):
                            tw = [tnrm2[hb]]
                            if i == 0 and prev_pt[tsl] is not None:
                                tw.append(prev_pt[tsl])
                            ttr = transp(b, psb[tsl][:, i * 128:(i + 1) * 128],
                                         pbf2[2 * hb + par][:, i * 128:(i + 1) * 128],
                                         st.idb[:], waits=tw,
                                         inc=(i == nkb - 1))
                        pbf_rd[2 * hb + par] = ttr
                        if hb == 0:
                            tev = b.op("act", lambda E, tsl=tsl, nkb=nkb: E.activation(
                                out=pts[tsl][:, 0:nkb * 128],
                                in_=psb[tsl][:, 0:nkb * 128],
                                func=AF.Copy), waits=[ttr])
                        else:
                            tev = b.op("dve", lambda E, tsl=tsl, nkb=nkb: E.tensor_copy(
                                pts[tsl][:, 0:nkb * 128],
                                psb[tsl][:, 0:nkb * 128]), waits=[ttr])
                        prev_pt[tsl] = tev
                        tevh[hb] = tev
                    for hb in (0, 1):
                        h = 2 * p + hb
                        tsl = hb
                        avmm = None
                        for i in range(nkb):
                            kb = kb0 + i
                            avw = [tevh[hb]]
                            if i == 0 and prev_o[hb] is not None:
                                avw.append(prev_o[hb])
                            avmm = mm(b, ps[0:64, (4 + hb) * 512:(4 + hb) * 512 + 128],
                                      vt[:, kb * 512 + h * 64:kb * 512 + h * 64 + 64],
                                      pts[tsl][:, i * 128:(i + 1) * 128],
                                      waits=avw,
                                      start=(i == 0), stop=(i == nkb - 1),
                                      inc=(i == nkb - 1), tp=(0, 0))
                        tevo = b.op("dve", lambda E, p=p, hb=hb, qt=qt: E.tensor_copy(
                            obf[p][64 * hb:64 * hb + 64, qt * 128:(qt + 1) * 128],
                            ps[0:64, (4 + hb) * 512:(4 + hb) * 512 + 128]),
                            waits=[avmm])
                        prev_o[hb] = tevo

                for p in range(4):
                    for qt in range(NQT):
                        kb0 = max(0, qt - 4)
                        nkb = qt - kb0 + 1
                        keys = nkb * 128
                        smm = [None, None]
                        for hb in (0, 1):
                            rows_u = slice(32 * hb, 32 * hb + 32)
                            rows_v = slice(64 + 32 * hb, 64 + 32 * hb + 32)
                            sb0 = 2 * hb
                            w0 = [prev_sc[hb]] if prev_sc[hb] else []
                            for k0 in range(0, keys, 512):
                                k1 = min(k0 + 512, keys)
                                kcols = slice(kb0 * 128 + k0, kb0 * 128 + k1)
                                obank = ps[:, sb0 * 512 + k0: sb0 * 512 + k1]
                                has_band = (qt >= 4 and k0 == 0)
                                has_caus = (k1 == keys)
                                tu = mm(b, obank,
                                        qba[p][rows_u, qt * 128:(qt + 1) * 128],
                                        kba[p][rows_u, kcols], waits=w0,
                                        start=True, stop=False, inc=True,
                                        tp=(32 * hb, 0))
                                w0 = []
                                tv2 = mm(b, obank,
                                         qba[p][rows_v, qt * 128:(qt + 1) * 128],
                                         kba[p][rows_v, kcols], waits=[tu],
                                         start=False,
                                         stop=not (has_band or has_caus),
                                         inc=True,
                                         tp=(64 + 32 * hb, 0))
                                smm[hb] = tv2
                                if has_band:
                                    smm[hb] = mm(b, ps[:, sb0 * 512:sb0 * 512 + 128],
                                                 st.idb[:], st.ml1f4b[:],
                                                 waits=[smm[hb]], start=False,
                                                 stop=not has_caus, inc=True)
                                if has_caus:
                                    smm[hb] = mm(
                                        b, ps[:, sb0 * 512 + keys - 128:sb0 * 512 + keys],
                                        st.idb[:], st.mcb[:],
                                        waits=[smm[hb]], start=False, stop=True,
                                        inc=True)
                        flush_tav()
                        texp = [None, None]
                        tnrm2 = [None, None]
                        for hb in (0, 1):
                            sb0 = 2 * hb
                            tmax = b.op("dve", lambda E, sb0=sb0, keys=keys, hb=hb:
                                        E.reduce_max(nm[:, hb:hb + 1],
                                                     ps[:, sb0 * 512:sb0 * 512 + keys],
                                                     axis=AX.X, negate=True),
                                        waits=[smm[hb]])
                            par = qt % 2
                            wexp = [tmax]
                            if pbf_rd[2 * hb + par] is not None:
                                wexp.append(pbf_rd[2 * hb + par])
                            texp[hb] = b.op("act", lambda E, sb0=sb0, keys=keys,
                                            hb=hb, par=par: E.activation(
                                out=pbf2[2 * hb + par][:, 0:keys],
                                in_=ps[:, sb0 * 512:sb0 * 512 + keys],
                                func=AF.Exp, bias=nm[:, hb:hb + 1],
                                accum_out=rs[:, hb:hb + 1]), waits=wexp)
                            prev_sc[hb] = texp[hb]
                            b.op("dve", lambda E, hb=hb: E.reciprocal(
                                rr[:, hb:hb + 1], rs[:, hb:hb + 1]), waits=[texp[hb]])
                            tnrm2[hb] = b.op("dve", lambda E, keys=keys, hb=hb,
                                             par=par: E.tensor_scalar_mul(
                                pbf2[2 * hb + par][:, 0:keys],
                                pbf2[2 * hb + par][:, 0:keys],
                                rr[:, hb:hb + 1]))
                        pend[0] = (p, qt, kb0, nkb, tnrm2)
                flush_tav()
                b.barrier()
                if DEBUG_D[0] and l == 0:
                    for p_ in range(4):
                        b.dma("g", D["dbg_o"][p_], obf[p_][:])
                    b.barrier()
                sp_.close()

                with (nc.psum_tensor(_u("psD4"), [128, 4096], F32) as ps4,
                      nc.sbuf_tensor(_u("woxD0"), [128, 1024], BF16) as wox0,
                      nc.sbuf_tensor(_u("woxD1"), [128, 1024], BF16) as wox1):
                    wox = [wox0, wox1]
                    bank_rd = [None] * 8
                    kcons = [None, None]
                    for c in range(NCH):
                        dmm = [None] * 8
                        for k in range(4):
                            wk_ = [kcons[k % 2]] if kcons[k % 2] else []
                            two = b.dma("g", wox[k % 2][:],
                                        D["l1f_wo"][l][k * 128:(k + 1) * 128, :],
                                        ch=f"woxD{k % 2}", waits=wk_)
                            for f in range(8):
                                w = [two]
                                if k == 0 and bank_rd[f] is not None:
                                    w.append(bank_rd[f])
                                dmm[f] = mm(b, ps4[:, f * 512:(f + 1) * 512],
                                            wox[k % 2][:, f * 128:(f + 1) * 128],
                                            obf[k][:, c * 512:(c + 1) * 512], waits=w,
                                            start=(k == 0), stop=(k == 3),
                                            inc=(k == 3 or f == 7))
                            kcons[k % 2] = dmm[7]
                        for f in range(8):
                            tres = b.op("dve", lambda E, f=f, c=c: E.tensor_add(
                                st.xf[f][:, c * 512:(c + 1) * 512],
                                st.xf[f][:, c * 512:(c + 1) * 512],
                                ps4[:, f * 512:(f + 1) * 512]),
                                waits=[dmm[f], spill[-1]])
                            bank_rd[f] = tres
                    b.barrier()

            for i in range(8):
                b.op("act", lambda E, i=i: E.activation(
                    out=st.xbf[i][:], in_=st.xf[i][:], func=AF.Copy))
            b.barrier()
            for i in range(4):
                b.op("dve", lambda E, i=i: E.memset(hsh0[i][:, 0:1], 0.0))
                b.op("dve", lambda E, i=i: E.tensor_copy(
                    hsh0[i][:, 1:512], st.xbf[4 + i][:, 0:511]))
            b.barrier()

            with ExitStack() as sf:
                ef = sf.enter_context
                ybf = [ef(nc.sbuf_tensor(_u(f"ybfD{i}"), [128, T], BF16)) for i in range(8)]
                w1a = [ef(nc.sbuf_tensor(_u(f"w1aD{j}"), [128, 8 * 512], BF16)) for j in range(2)]
                w1g = [ef(nc.sbuf_tensor(_u(f"w1gD{j}"), [128, 8 * 512], BF16)) for j in range(2)]
                w2r = [ef(nc.sbuf_tensor(_u(f"w2rD{j}"), [128, 1024], BF16)) for j in range(2)]
                hp = [ef(nc.sbuf_tensor(_u(f"hpD{jp}"), [128, T], BF16)) for jp in range(4)]
                gel = ef(nc.sbuf_tensor(_u("gelD"), [128, 512], F32))
                ps2 = ef(nc.psum_tensor(_u("psD2"), [128, 4096], F32))

                def bank2(i):
                    return ps2[:, i * 512:(i + 1) * 512]

                bank_rd2 = {}
                prev_tt_d = [None]
                last_w2 = [None]
                tgb = []
                for i in range(8):
                    tgb.append(b.dma("sp", lngc[:, i:i + 1],
                                     D["l1f_lng"][l][i * 128:(i + 1) * 128]))
                    tgb.append(b.dma("sp", lnbc[:, i:i + 1],
                                     D["l1f_lnb"][l][i * 128:(i + 1) * 128]))
                    tgb.append(b.dma("sp", b2c[:, i:i + 1],
                                     D["l1f_b2"][l][i * 128:(i + 1) * 128]))
                for i in range(22):
                    tgb.append(b.dma("sp", b1ac[:, i:i + 1],
                                     D["l1f_b1a"][l][i * 128:(i + 1) * 128]))
                    tgb.append(b.dma("sp", b1gc[:, i:i + 1],
                                     D["l1f_b1g"][l][i * 128:(i + 1) * 128]))
                b.wait("dve", tgb)
                b.wait("act", tgb)

                emit_ln(b, st, ps2[:, 0:2048], mode="inner",
                        gb=(lngc, lnbc), ybf=ybf,
                        src_fn=lambda i, c: shifted_bf(i, c * 512, 512),
                        extra=(ps2[:, 2048:2560], ps2[:, 2560:3072]))
                b.barrier()

                w1cons = [None, None]
                w2cons = [None, None]
                cc0 = 0
                for ci, cw in enumerate(CCW):
                    njp = cw // 128
                    slt = ci % 2
                    wl = [w1cons[slt]] if w1cons[slt] else []
                    t1a = b.dma("g", w1a[slt][:, 0:8 * cw].rearrange(
                        "p (k j) -> p k j", k=8),
                        D["l1f_w1"].rearrange("l (k p) j -> l p k j", p=128)
                        [l][:, :, cc0:cc0 + cw], ch=f"w1aD{slt}", waits=wl)
                    t1g = b.dma("g", w1g[slt][:, 0:8 * cw].rearrange(
                        "p (k j) -> p k j", k=8),
                        D["l1f_w1"].rearrange("l (k p) j -> l p k j", p=128)
                        [l][:, :, HP + cc0:HP + cc0 + cw], ch=f"w1gD{slt}", waits=wl)
                    lastw1 = None
                    gu_done = []
                    for jp in range(njp):
                        gi = cc0 // 128 + jp
                        for c in range(NCH):
                            rot = (jp * NCH + c) % 2
                            bA, bB = rot, 2 + rot
                            wA = [t1a]
                            if bank_rd2.get(bA) is not None:
                                wA.append(bank_rd2[bA])
                            amm = None
                            for k in range(8):
                                amm = mm(b, bank2(bA),
                                         w1a[slt][:, k * cw + jp * 128:k * cw + jp * 128 + 128],
                                         ybf[k][:, c * 512:(c + 1) * 512], waits=wA,
                                         start=(k == 0), stop=(k == 7), inc=(k == 7))
                                wA = []
                            wB = [t1g]
                            if bank_rd2.get(bB) is not None:
                                wB.append(bank_rd2[bB])
                            gmm = None
                            for k in range(8):
                                gmm = mm(b, bank2(bB),
                                         w1g[slt][:, k * cw + jp * 128:k * cw + jp * 128 + 128],
                                         ybf[k][:, c * 512:(c + 1) * 512], waits=wB,
                                         start=(k == 0), stop=(k == 7), inc=(k == 7))
                                wB = []
                            lastw1 = gmm
                            tgl = b.op("act", lambda E, bB=bB, gi=gi: E.activation(
                                out=gel[:], in_=bank2(bB), func=AF.Gelu,
                                bias=b1gc[:, gi:gi + 1]),
                                waits=[gmm] + ([prev_tt_d[0]] if prev_tt_d[0] else []))
                            bank_rd2[bB] = tgl
                            th = b.op("dve", lambda E, jp=jp, c=c, bA=bA, gi=gi:
                                      E.scalar_tensor_tensor(
                                hp[jp][:, c * 512:(c + 1) * 512], bank2(bA),
                                b1ac[:, gi:gi + 1], gel[:], ALU.add, ALU.mult),
                                waits=[amm, tgl] + ([last_w2[0]] if last_w2[0] else []))
                            bank_rd2[bA] = th
                            prev_tt_d[0] = th
                            gu_done.append(th)
                    w1cons[slt] = lastw1
                    for c in range(NCH):
                        dmm = None
                        for jp in range(njp):
                            sl2 = jp % 2
                            w2w = [w2cons[sl2]] if w2cons[sl2] else []
                            t2r = b.dma("g", w2r[sl2][:],
                                        D["l1f_w2"][l][cc0 + jp * 128:cc0 + (jp + 1) * 128, :],
                                        ch=f"w2rD{sl2}", waits=w2w)
                            for f in range(8):
                                w = [t2r] + gu_done
                                if jp == 0 and bank_rd2.get(f) is not None:
                                    w.append(bank_rd2[f])
                                dmm = mm(b, bank2(f), w2r[sl2][:, f * 128:(f + 1) * 128],
                                         hp[jp][:, c * 512:(c + 1) * 512], waits=w,
                                         start=(jp == 0), stop=(jp == njp - 1),
                                         inc=(jp == njp - 1 or f == 7))
                            w2cons[sl2] = dmm
                        last_w2[0] = dmm
                        for f in range(8):
                            if ci == len(CCW) - 1:
                                tres = b.op("dve", lambda E, f=f, c=c: E.scalar_tensor_tensor(
                                    st.xf[f][:, c * 512:(c + 1) * 512], bank2(f),
                                    b2c[:, f:f + 1],
                                    st.xf[f][:, c * 512:(c + 1) * 512],
                                    ALU.add, ALU.add), waits=[dmm])
                            else:
                                tres = b.op("dve", lambda E, f=f, c=c: E.tensor_add(
                                    st.xf[f][:, c * 512:(c + 1) * 512],
                                    st.xf[f][:, c * 512:(c + 1) * 512],
                                    bank2(f)), waits=[dmm])
                            bank_rd2[f] = tres
                    cc0 += cw
                b.barrier()
            if l == 0:
                for i in range(8):
                    b.op("act", lambda E, i=i: E.activation(
                        out=st.xbf[i][:], in_=st.xf[i][:], func=AF.Copy))
                b.barrier()

        for l in range(2):
            _run_layer(l)

        with (nc.psum_tensor(_u("psD3"), [128, 3072], F32) as ps3,
              nc.sbuf_tensor(_u("xinD0"), [128, T], F32) as xin0,
              nc.sbuf_tensor(_u("xinD1"), [128, T], F32) as xin1):
            xin = [xin0, xin1]
            tadd = []
            for i in range(8):
                tl = b.dma("sp", xin[i % 2][:], D["xin_spill"][i], ch="spill",
                           waits=(tadd[-1:] if tadd else []) + [spill[-1]])
                ta = b.op("dve", lambda E, i=i: E.tensor_add(
                    st.xf[i][:], st.xf[i][:], xin[i % 2][:]), waits=[tl])
                tadd.append(ta)
            b.barrier()
            emit_ln(b, st, ps3[:, 0:2048], mode="std",
                    extra=(ps3[:, 2048:2560], ps3[:, 2560:3072]))
            b.barrier()



def _run_launch(blocks, xt_cores, W, C, T=2048, NSEQ=4):
    nc, D = build_program(blocks, T, NSEQ)
    names = {k for k in D if k != "yt" and k != "xin_spill"}
    shared = {}
    for k in names:
        if k == "xt":
            continue
        shared[k] = W[k] if k in W else C[k]
    in_maps = []
    for c in range(NCORES):
        m = dict(shared)
        m["xt"] = xt_cores[c]
        in_maps.append(m)
    res = run_bass_kernel_spmd(nc, in_maps, core_ids=list(range(NCORES)))
    return np.stack([np.asarray(r["yt"]) for r in res.results])


def kernel(**inputs):
    T, NSEQ = 2048, 4
    W = pack_weights(inputs)
    C = host_constants(T)
    x = np.asarray(inputs["L0_attn_x"], dtype=np.float32)
    xt = np.ascontiguousarray(x.transpose(0, 2, 1))
    xt_cores = [xt[c * NSEQ:(c + 1) * NSEQ] for c in range(NCORES)]
    yt = _run_launch(["A", "B", "C", "D"], xt_cores, W, C, T, NSEQ)
    y = np.concatenate([yt[c] for c in range(NCORES)], axis=0)
    return np.ascontiguousarray(y.transpose(0, 2, 1))
